# version 22
# baseline (speedup 1.0000x reference)
import numpy as np
import ml_dtypes
import concourse.bass as bass
import concourse.mybir as mybir
from concourse.bass_utils import run_bass_kernel_spmd
from contextlib import ExitStack

F32 = mybir.dt.float32
BF16 = mybir.dt.bfloat16
I32 = mybir.dt.int32
I8 = mybir.dt.int8
AF = mybir.ActivationFunctionType
ALU = mybir.AluOpType
bfloat16 = ml_dtypes.bfloat16

H = 2048
I5504 = 5504
HK = 16
OC = 16
IC = 43
T = 1024
W = 1536
COLS = T + W + 1
NSEG = 5
SEG = 512
EPS = 1e-5
NT = 258
NTL = NT // 2
BM = 0x5A170000


def _blk_base(t):
    if t < 172:
        ic, r = divmod(t, 4)
        return (0 if r < 2 else 688) + 16 * ic + 8 * (r % 2)
    return 1376 + 8 * (t - 172)


def _sl(buf, k, w, a=0, b=None):
    if b is None:
        b = w
    return buf[:, k * w + a:k * w + b]


def build_conv_nc(lut):
    nc = bass.Bass("TRN2", target_bir_lowering=False, detect_race_conditions=False)
    d_g = nc.dram_tensor("g1_gidx", [IC, 256, 128], I32, kind="ExternalInput").ap()
    d_u = nc.dram_tensor("g1_uidx", [IC, 256, 128], I32, kind="ExternalInput").ap()
    d_d = nc.dram_tensor("g1_didx", [2, I5504, 128], I32, kind="ExternalInput").ap()
    o_g = nc.dram_tensor("g1_wg", [IC, 256, 128], BF16, kind="ExternalOutput").ap()
    o_u = nc.dram_tensor("g1_wu", [IC, 256, 128], BF16, kind="ExternalOutput").ap()
    o_d = nc.dram_tensor("g1_wd", [2, I5504, 128], BF16, kind="ExternalOutput").ap()
    NF = IC * 2 * 128
    with ExitStack() as ctx:
        s_i = [ctx.enter_context(nc.sbuf_tensor(f"s_i{k}", [128, NF], I32))
               for k in range(2)]
        s_b = ctx.enter_context(nc.sbuf_tensor("s_b", [128, NF], BF16))
        s_m = ctx.enter_context(nc.sbuf_tensor("s_m", [128, NF], BF16))
        s_a = [ctx.enter_context(nc.sbuf_tensor(f"s_a{k}", [128, NF], BF16))
               for k in range(2)]
        ld = [ctx.enter_context(nc.semaphore(f"ld{k}")) for k in range(2)]
        dv = ctx.enter_context(nc.semaphore("dv"))
        st = [ctx.enter_context(nc.semaphore(f"st{k}")) for k in range(2)]
        block = ctx.enter_context(nc.Block())

        def view4(ap_, a, b):
            return ap_.rearrange("p (x y i) -> p x y i", x=a, y=b)

        srcs = [d_g.rearrange("ic (rr p) i -> p ic rr i", p=128),
                d_u.rearrange("ic (rr p) i -> p ic rr i", p=128),
                d_d.rearrange("b (jj p) i -> p b jj i", p=128)]
        dsts = [o_g.rearrange("ic (rr p) i -> p ic rr i", p=128),
                o_u.rearrange("ic (rr p) i -> p ic rr i", p=128),
                o_d.rearrange("b (jj p) i -> p b jj i", p=128)]
        shp = [(IC, 2), (IC, 2), (2, 43)]

        @block.sync
        def _(sync):
            for n in range(3):
                if n >= 2:
                    sync.wait_ge(dv, 2 * (n - 2) + 1)
                sync.dma_start(out=view4(s_i[n % 2].ap(), *shp[n]),
                               in_=srcs[n]).then_inc(ld[n % 2], 16)

        @block.vector
        def _(v):
            for n in range(3):
                v.wait_ge(ld[n % 2], 16 * (n // 2 + 1))
                v.tensor_copy(out=s_b[:], in_=s_i[n % 2][:]).then_inc(dv, 1)
                if n >= 2:
                    v.wait_ge(st[n % 2], 16)
                acc = s_a[n % 2]
                last = None
                for j in range(16):
                    if j == 0:
                        v.tensor_scalar(out=acc[:], in0=s_b[:], scalar1=0.0,
                                        scalar2=float(lut[0]),
                                        op0=ALU.is_equal, op1=ALU.mult)
                    else:
                        v.tensor_scalar(out=s_m[:], in0=s_b[:], scalar1=float(j),
                                        scalar2=float(lut[j]),
                                        op0=ALU.is_equal, op1=ALU.mult)
                        last = v.tensor_tensor(out=acc[:], in0=acc[:],
                                               in1=s_m[:], op=ALU.add)
                last.then_inc(dv, 1)

        @block.gpsimd
        def _(g):
            for n in range(3):
                g.wait_ge(dv, 2 * (n + 1))
                g.dma_start(out=dsts[n],
                            in_=view4(s_a[n % 2].ap(), *shp[n])
                            ).then_inc(st[n % 2], 16)
            g.wait_ge(st[0], 32)
            g.wait_ge(st[1], 16)
    return nc


DEBUG = False


def build_main_nc(lut):
    nc = bass.Bass("TRN2", target_bir_lowering=False, detect_race_conditions=False)
    d_x = nc.dram_tensor("x_fm", [H, COLS], F32, kind="ExternalInput").ap()
    d_wk = nc.dram_tensor("wk_blk", [OC, H, 128], F32, kind="ExternalInput").ap()
    d_wv = nc.dram_tensor("wv_blk", [OC, H, 128], F32, kind="ExternalInput").ap()
    d_wr = nc.dram_tensor("wr_blk", [OC, H, 128], F32, kind="ExternalInput").ap()
    d_wo = nc.dram_tensor("wo_blk", [OC, H, 128], F32, kind="ExternalInput").ap()
    d_ci = nc.dram_tensor("cidx", [NTL, 128, 1024], I8,
                          kind="ExternalInput").ap()
    d_pm = nc.dram_tensor("pmeta", [1, 4], I32, kind="ExternalInput").ap()
    d_pt = nc.dram_tensor("ptab", [1, NTL + 2], I32,
                          kind="ExternalInput").ap()
    d_ws = nc.dram_tensor("wscr", [NT * 8, 128, 128], BF16,
                          kind="Internal", addr_space="Shared").ap()
    sprog = nc.dram_tensor("sprog", [1, 2], I32, kind="Internal",
                           addr_space="Shared").ap()
    d_tmk = nc.dram_tensor("tmk", [128, 16], F32, kind="ExternalInput").ap()
    d_tmv = nc.dram_tensor("tmv", [128, 16], F32, kind="ExternalInput").ap()
    d_tmr = nc.dram_tensor("tmr", [128, 16], F32, kind="ExternalInput").ap()
    d_u = nc.dram_tensor("ucol", [128, 16], F32, kind="ExternalInput").ap()
    d_td = nc.dram_tensor("tdcol", [128, 16], F32, kind="ExternalInput").ap()
    d_ones = nc.dram_tensor("ones", [128, 128], F32, kind="ExternalInput").ap()
    d_out = nc.dram_tensor("out_fm", [H, T], F32, kind="ExternalOutput").ap()
    c_wk = nc.dram_tensor("wkbf", [OC, H, 128], BF16, kind="Internal").ap()
    c_wv = nc.dram_tensor("wvbf", [OC, H, 128], BF16, kind="Internal").ap()
    if DEBUG:
        dbg_xnm = nc.dram_tensor("dbg_xnm", [128, HK * (T + 1)], BF16,
                                 kind="ExternalOutput").ap()
        dbg_kvm = nc.dram_tensor("dbg_kvm", [128, OC * T], BF16,
                                 kind="ExternalOutput").ap()
        dbg_attn = nc.dram_tensor("dbg_attn", [128, OC * T], BF16,
                                  kind="ExternalOutput").ap()
        dbg_s = nc.dram_tensor("dbg_s", [128, HK * T], BF16,
                               kind="ExternalOutput").ap()
        dbg_hact = nc.dram_tensor("dbg_hact", [128, IC * SEG], BF16,
                                  kind="ExternalOutput").ap()

    xr = d_x.rearrange("(hk p) c -> hk p c", p=128)
    outr = d_out.rearrange("(hk p) c -> hk p c", p=128)

    def oblk(d):
        return d.rearrange("o (hk p) i -> o p hk i", p=128)

    wsv = d_ws.rearrange("b p j -> p b j")

    with ExitStack() as outer:
        arena = outer.enter_context(nc.sbuf_tensor("arena", [128, 49152], BF16))
        kvm = arena[:, 0:16384]
        kbuf = arena[:, 16384:32768]
        xm = arena[:, 32768:49152]
        xmr = arena[:, 16384:32768]
        sout = arena[:, 32768:49152]
        hact = arena[:, 0:22016]
        xnm = outer.enter_context(nc.sbuf_tensor("xnm", [128, HK * (T + 1)],
                                                 BF16))
        carry = outer.enter_context(nc.sbuf_tensor("carry", [128, 16], F32))
        carry3 = outer.enter_context(nc.sbuf_tensor("carry3", [128, 16], F32))
        cons = outer.enter_context(nc.sbuf_tensor("cons", [128, 128], F32))
        tmk = outer.enter_context(nc.sbuf_tensor("tmk_s", [128, 16], F32))
        tmv = outer.enter_context(nc.sbuf_tensor("tmv_s", [128, 16], F32))
        tmr = outer.enter_context(nc.sbuf_tensor("tmr_s", [128, 16], F32))
        tmk1 = outer.enter_context(nc.sbuf_tensor("tmk1", [128, 16], F32))
        tmv1 = outer.enter_context(nc.sbuf_tensor("tmv1", [128, 16], F32))
        tmr1 = outer.enter_context(nc.sbuf_tensor("tmr1", [128, 16], F32))
        ucol = outer.enter_context(nc.sbuf_tensor("ucol_s", [128, 16], F32))
        tdc = outer.enter_context(nc.sbuf_tensor("tdc", [128, 16], F32))
        edc = outer.enter_context(nc.sbuf_tensor("edc", [128, 16], F32))
        wcol = outer.enter_context(nc.sbuf_tensor("wcol", [128, 16], F32))

        sem = {}
        for name in ("ldx", "ldw", "pex", "pew", "peb", "dvst", "dvbc", "dvmx",
                     "dvev", "dvat", "acx", "acst", "acc", "acs", "dvh", "st",
                     "stg", "phb", "dvn", "dvq", "pub", "pol"):
            sem[name] = outer.enter_context(nc.semaphore(name))
        C = {k: 0 for k in sem}
        PHB = [0]

        class DS:

            def __init__(self, name, nbuf):
                self.s = [outer.enter_context(nc.semaphore(f"{name}{k}"))
                          for k in range(nbuf)]
                self.nb = nbuf

            def dma(self, q, out, in_, idx):
                return q.dma_start(out=out, in_=in_).then_inc(
                    self.s[idx % self.nb], 16)

            def wait(self, eng, idx):
                eng.wait_ge(self.s[idx % self.nb], 16 * (idx // self.nb + 1))

            def wait_all(self, eng, total):
                for k in range(self.nb):
                    n = total // self.nb + (1 if k < total % self.nb else 0)
                    eng.wait_ge(self.s[k], 16 * n)

        NCVI = 3
        NCVA = 3
        CVL = DS("cvl", NCVI)
        CVS = DS("cvs", NCVA)
        cvc = outer.enter_context(nc.semaphore("cvc"))
        cvi = [outer.enter_context(nc.sbuf_tensor(f"cvi{k}", [128, 1024], I8))
               for k in range(NCVI)]
        cvb = outer.enter_context(nc.sbuf_tensor("cvb", [128, 1024], BF16))
        cvm = outer.enter_context(nc.sbuf_tensor("cvm", [128, 1024], BF16))
        cva = [outer.enter_context(nc.sbuf_tensor(f"cva{k}", [128, 1024], BF16))
               for k in range(NCVA)]
        s_pm = outer.enter_context(nc.sbuf_tensor("s_pm", [1, 4], I32))
        s_pt = outer.enter_context(nc.sbuf_tensor("s_pt", [1, NTL + 2], I32))
        s_pp = outer.enter_context(nc.sbuf_tensor("s_pp", [1, 2], I32))
        SPR = {"r": None}

        def spin(sync, cnt):
            if SPR["r"] is None:
                ra = sync.alloc_register("spa")
                rb = sync.alloc_register("spb")
                rc = sync.alloc_register("spc")
                sync.dma_start(out=s_pp[:], in_=d_pm[0:1, 1:3]
                               ).then_inc(sem["pol"], 16)
                sync.wait_ge(sem["pol"], 16)
                sync.reg_mov(ra, 0)
                sync.reg_mov(rb, 0)
                SPR["r"] = (ra, rb, rc)
            ra, rb, rc = SPR["r"]
            tgt = BM + cnt

            def check():
                sync.reg_alu(rc, ra, rb, ALU.min)
                sync.reg_alu(rc, rc, tgt, ALU.is_ge)
                sync.reg_alu(rc, rc, 1, ALU.bitwise_xor)
            check()
            with sync.While(rc):
                sync.dma_start(out=s_pp[:], in_=sprog).then_inc(sem["pol"], 16)
                sync.reg_load(ra, s_pp[0:1, 0:1])
                sync.reg_load(rb, s_pp[0:1, 1:2])
                check()

        class Converter:

            def __init__(self):
                self.last = {}
                for ic in range(IC):
                    self.last[("g", ic)] = 4 * ic + 1
                    self.last[("u", ic)] = 4 * ic + 3
                for hc in range(HK):
                    self.last[("d", hc)] = 172 + (43 * hc + 42) // 8
                self.nd = 0
                self.ng = 0
                self.flushed = False
                self.rpar = None

            def _setup_gp(self, g):
                if self.rpar is not None:
                    return
                g.dma_start(out=s_pm[:], in_=d_pm).then_inc(sem["pub"], 16)
                g.dma_start(out=s_pt[:], in_=d_pt).then_inc(sem["pub"], 16)
                g.wait_ge(sem["pub"], 32)
                self.rpar = g.alloc_register("rpar")
                g.reg_load(self.rpar, s_pm[0:1, 0:1])

            def emit_dve(self, v, k):
                for _ in range(k):
                    j = self.nd
                    if j >= NTL:
                        return
                    CVL.wait(v, j)
                    if j >= NCVA:
                        CVS.wait(v, j - NCVA)
                    acc = cva[j % NCVA]
                    v.tensor_copy(out=cvb[:], in_=cvi[j % NCVI][:])
                    lastop = None
                    for i in range(16):
                        if i == 0:
                            v.tensor_scalar(out=acc[:], in0=cvb[:],
                                            scalar1=0.0, scalar2=float(lut[0]),
                                            op0=ALU.is_equal, op1=ALU.mult)
                        else:
                            v.tensor_scalar(out=cvm[:], in0=cvb[:],
                                            scalar1=float(i),
                                            scalar2=float(lut[i]),
                                            op0=ALU.is_equal, op1=ALU.mult)
                            lastop = v.tensor_tensor(out=acc[:], in0=acc[:],
                                                     in1=cvm[:], op=ALU.add)
                    lastop.then_inc(cvc, 1)
                    self.nd += 1

            def _store(self, g, js):
                g.wait_ge(cvc, js + 1)
                src = cva[js % NCVA].ap().rearrange("p (a i) -> p a i", a=8)
                b0 = _blk_base(2 * js)
                b1 = _blk_base(2 * js + 1)
                with g.If_eq(self.rpar, 0):
                    CVS.dma(g, wsv[:, b0:b0 + 8, :], src, js)
                with g.Else():
                    CVS.dma(g, wsv[:, b1:b1 + 8, :], src, js)
                CVS.wait(g, js)
                with g.If_eq(self.rpar, 0):
                    g.dma_start(out=sprog[0:1, 0:1],
                                in_=s_pt[0:1, js + 1:js + 2]
                                ).then_inc(sem["pub"], 16)
                with g.Else():
                    g.dma_start(out=sprog[0:1, 1:2],
                                in_=s_pt[0:1, js + 1:js + 2]
                                ).then_inc(sem["pub"], 16)

            def emit_gp(self, g, upto):
                self._setup_gp(g)
                while self.ng < min(upto, NTL):
                    j = self.ng
                    if j >= NCVI - 1:
                        g.wait_ge(cvc, j - (NCVI - 1))
                    CVL.dma(g, cvi[j % NCVI][:], d_ci[j], j)
                    if j >= NCVA - 1:
                        self._store(g, j - (NCVA - 1))
                    self.ng += 1

            def flush_gp(self, g):
                for js in range(NTL - (NCVA - 1), NTL):
                    self._store(g, js)

        CONV = Converter()

        XS = DS("xs", 3)
        WS = DS("ws", 3)
        BS = DS("bs", 2)
        SG = DS("sg2", 2)
        FS = DS("fs", 2)
        GS = DS("gs", 2)
        US = DS("us", 2)
        DSm = DS("ds", 2)
        YS = DS("ys", 2)
        OS = DS("os", 2)

        def bw(eng):
            if PHB[0]:
                eng.wait_ge(sem["phb"], PHB[0])

        with ExitStack() as ph:
            blk = ph.enter_context(nc.Block())

            @blk.sync
            def _(sync):
                for d, s in ((d_ones, cons), (d_tmk, tmk), (d_tmv, tmv),
                             (d_tmr, tmr), (d_u, ucol), (d_td, tdc)):
                    sync.dma_start(out=s[:], in_=d).then_inc(sem["ldx"], 16)
            C["ldx"] += 96

            @blk.scalar
            def _(sc):
                sc.wait_ge(sem["ldx"], 96)
                sc.activation(out=edc[:], in_=tdc[:], func=AF.Exp)
                sc.activation(out=wcol[:], in_=edc[:], func=AF.Exp,
                              scale=-1.0).then_inc(sem["phb"], 1)

            @blk.vector
            def _(v):
                v.wait_ge(sem["ldx"], 96)
                v.memset(carry[:], 0.0)
                last = None
                for a, b in ((tmk, tmk1), (tmv, tmv1), (tmr, tmr1)):
                    last = v.tensor_scalar(out=b[:], in0=a[:], scalar1=-1.0,
                                           scalar2=1.0, op0=ALU.mult,
                                           op1=ALU.add)
                CONV.emit_dve(v, 16)
                last.then_inc(sem["phb"], 1)

            @blk.gpsimd
            def _(g):
                CONV.emit_gp(g, CONV.nd + 1)
            PHB[0] += 2

        ACCB = C["acc"]
        with ExitStack() as ph:
            c0f = [ph.enter_context(nc.sbuf_tensor(f"c0f{k}", [128, 1024],
                                                   F32)) for k in range(2)]
            csrc = [d_wk, d_wv]
            cdst = [c_wk, c_wv]
            xsl = [ph.enter_context(nc.sbuf_tensor(f"xsl{k}", [128, SEG + 1],
                                                   F32)) for k in range(3)]
            sqs = [ph.enter_context(nc.sbuf_tensor(f"sqs{k}", [128, SEG + 1],
                                                   F32)) for k in range(2)]
            xns = ph.enter_context(nc.sbuf_tensor("xns", [128, HK * (SEG + 1)],
                                                  BF16))
            tsc = ph.enter_context(nc.sbuf_tensor("tsc", [128, SEG], BF16))
            wsl = [ph.enter_context(nc.sbuf_tensor(f"wsl{k}", [128, H], BF16))
                   for k in range(3)]
            lbf = [ph.enter_context(nc.sbuf_tensor(f"lbf{k}", [128, SEG + 1],
                                                   F32)) for k in range(2)]
            xnf = ph.enter_context(nc.sbuf_tensor("xnf", [128, SEG + 1], F32))
            mrow = ph.enter_context(nc.sbuf_tensor("mrow", [1, SEG + 1], F32))
            vrow = ph.enter_context(nc.sbuf_tensor("vrow", [1, SEG + 1], F32))
            rrow = ph.enter_context(nc.sbuf_tensor("rrow", [1, SEG + 1], F32))
            mb = ph.enter_context(nc.sbuf_tensor("mb", [128, SEG + 1], F32))
            rb = ph.enter_context(nc.sbuf_tensor("rb", [128, SEG + 1], F32))
            p_sx = ph.enter_context(nc.psum_tensor("p_sx", [1, SEG + 1], F32))
            p_sq = ph.enter_context(nc.psum_tensor("p_sq", [1, SEG + 1], F32))
            p_bc = ph.enter_context(nc.psum_tensor("p_bc", [128, SEG + 1], F32))
            p_kv = [ph.enter_context(nc.psum_tensor(f"p_kv{k}", [128, SEG],
                                                    F32)) for k in range(2)]
            blk = ph.enter_context(nc.Block())

            @blk.sync
            def _(sync):
                bw(sync)
                for s in range(NSEG):
                    for j in range(32):
                        hk = j % 16
                        nx = 32 * s + j
                        if nx >= 3:
                            sp, jp = divmod(nx - 3, 32)
                            if jp < 16:
                                sync.wait_ge(sem["acx"], 16 * sp + jp + 1)
                                sync.wait_ge(sem["pex"], 16 * sp + jp + 1)
                            else:
                                sync.wait_ge(sem["dvn"], 16 * sp + jp - 16 + 1)
                        XS.dma(sync, xsl[nx % 3][:],
                               xr[hk, :, SEG * s:SEG * s + SEG + 1], nx)
                    if s == 0:
                        for n in range(64):
                            w, oc, h = n // 32, (n // 2) % 16, n % 2
                            if n >= 2:
                                sync.wait_ge(sem["acc"], ACCB + n - 1)
                            BS.dma(sync,
                                   c0f[n % 2].ap().rearrange(
                                       "p (hk i) -> p hk i", hk=8),
                                   oblk(csrc[w])[oc][:, 8 * h:8 * h + 8, :],
                                   n)
                    for w, cw in ((0, c_wk), (1, c_wv)):
                        for oc in range(OC):
                            if s == 0:
                                continue
                            gidx = 32 * s + 16 * w + oc
                            sync.wait_ge(sem["pew"], gidx - 2)
                            if s == 1 and gidx - 32 < 3:
                                m = gidx - 32
                                mp = 31 - ((31 - (2 + m)) % 3)
                                SG.wait(sync, 2 * mp + 1)
                            SG.wait(sync, 2 * (16 * w + oc) + 1)
                            WS.dma(sync,
                                   wsl[gidx % 3].ap().rearrange(
                                       "p (hk i) -> p hk i", hk=HK),
                                   oblk(cw)[oc], gidx - 32)

            @blk.scalar
            def _(sc):
                bw(sc)

                def cast(n):
                    g, h = divmod(n, 2)
                    BS.wait(sc, n)
                    if n >= 6:
                        SG.wait(sc, n - 6)
                    if g >= 3:
                        sc.wait_ge(sem["pew"], g - 2)
                    sc.activation(out=wsl[g % 3][:, 1024 * h:1024 * h + 1024],
                                  in_=c0f[n % 2][:],
                                  func=AF.Identity).then_inc(sem["acc"], 1)
                for s in range(NSEG):
                    for hk in range(HK):
                        n = 16 * s + hk
                        nx = 32 * s + hk
                        XS.wait(sc, nx)
                        if n >= 2:
                            sc.wait_ge(sem["pex"], n - 1)
                        sc.activation(out=sqs[n % 2][:], in_=xsl[nx % 3][:],
                                      func=AF.Square).then_inc(sem["acx"], 1)
                    sc.wait_ge(sem["dvst"], s + 1)
                    sc.activation(out=rrow[:], in_=vrow[:], func=AF.Ln)
                    sc.activation(out=rrow[:], in_=rrow[:], func=AF.Exp,
                                  scale=-0.5).then_inc(sem["acst"], 1)
                    if s == 0:
                        for n in range(4):
                            cast(n)
                    for oc in range(OC):
                        gidx = 32 * s + oc
                        sc.wait_ge(sem["pew"], gidx + 1)
                        sc.activation(out=_sl(kbuf, oc, SEG),
                                      in_=p_kv[gidx % 2][:],
                                      func=AF.Identity).then_inc(sem["dvev"], 1)
                        if s == 0:
                            cast(2 * oc + 4)
                            cast(2 * oc + 5)
                    if s == 0:
                        for n in range(36, 64):
                            cast(n)
            C["acx"] += NSEG * 16
            C["acst"] += NSEG
            C["acc"] += 64

            @blk.tensor
            def _(t):
                bw(t)
                for s in range(NSEG):
                    for hk in range(HK):
                        n = 16 * s + hk
                        nx = 32 * s + hk
                        t.wait_ge(sem["acx"], n + 1)
                        if s >= 1 and hk == 0:
                            t.wait_ge(sem["dvst"], s)
                        st_ = (hk == 0)
                        sp_ = (hk == HK - 1)
                        t.matmul(out=p_sx[:, 0:512], lhsT=cons[:, 0:1],
                                 rhs=xsl[nx % 3][:, 0:512], start=st_, stop=sp_)
                        t.matmul(out=p_sx[:, 512:513], lhsT=cons[:, 0:1],
                                 rhs=xsl[nx % 3][:, 512:513], start=st_, stop=sp_)
                        t.matmul(out=p_sq[:, 0:512], lhsT=cons[:, 0:1],
                                 rhs=sqs[n % 2][:, 0:512], start=st_, stop=sp_)
                        t.matmul(out=p_sq[:, 512:513], lhsT=cons[:, 0:1],
                                 rhs=sqs[n % 2][:, 512:513], start=st_,
                                 stop=sp_).then_inc(sem["pex"], 1)
                    t.wait_ge(sem["dvst"], s + 1)
                    if s >= 1:
                        t.wait_ge(sem["dvbc"], 2 * s)
                    t.matmul(out=p_bc[:, 0:512], lhsT=cons[0:1, :],
                             rhs=mrow[:, 0:512], start=True, stop=True)
                    t.matmul(out=p_bc[:, 512:513], lhsT=cons[0:1, :],
                             rhs=mrow[:, 512:513], start=True,
                             stop=True).then_inc(sem["peb"], 1)
                    t.wait_ge(sem["acst"], s + 1)
                    t.wait_ge(sem["dvbc"], 2 * s + 1)
                    t.matmul(out=p_bc[:, 0:512], lhsT=cons[0:1, :],
                             rhs=rrow[:, 0:512], start=True, stop=True)
                    t.matmul(out=p_bc[:, 512:513], lhsT=cons[0:1, :],
                             rhs=rrow[:, 512:513], start=True,
                             stop=True).then_inc(sem["peb"], 1)
                    for w in range(2):
                        for oc in range(OC):
                            gidx = 32 * s + 16 * w + oc
                            if s == 0:
                                t.wait_ge(sem["acc"],
                                          ACCB + 2 * (16 * w + oc) + 2)
                            else:
                                WS.wait(t, gidx - 32)
                            t.wait_ge(sem["dvmx"], 2 * s + w + 1)
                            if gidx >= 2:
                                t.wait_ge(sem["dvev"], gidx - 1)
                            for hk in range(HK):
                                mm = t.matmul(out=p_kv[gidx % 2][:],
                                              lhsT=_sl(wsl[gidx % 3], hk, 128),
                                              rhs=_sl(xm, hk, SEG),
                                              start=(hk == 0),
                                              stop=(hk == HK - 1))
                            mm.then_inc(sem["pew"], 1)
            C["pex"] += NSEG * 16
            C["peb"] += NSEG * 2
            C["pew"] += NSEG * 32

            @blk.vector
            def _(v):
                bw(v)
                CONV.emit_dve(v, 3)
                for s in range(NSEG):
                    v.wait_ge(sem["pex"], 16 * (s + 1))
                    v.tensor_scalar(out=mrow[:], in0=p_sx[:], scalar1=1.0 / H,
                                    scalar2=None, op0=ALU.mult)
                    v.scalar_tensor_tensor(out=vrow[:], in0=mrow[:],
                                           scalar=-1.0, in1=mrow[:],
                                           op0=ALU.mult, op1=ALU.mult)
                    v.tensor_scalar(out=rrow[:], in0=p_sq[:], scalar1=1.0 / H,
                                    scalar2=EPS, op0=ALU.mult, op1=ALU.add)
                    v.tensor_tensor(out=vrow[:], in0=rrow[:], in1=vrow[:],
                                    op=ALU.add).then_inc(sem["dvst"], 1)
                    v.wait_ge(sem["peb"], 2 * s + 1)
                    v.tensor_copy(out=mb[:], in_=p_bc[:]).then_inc(sem["dvbc"],
                                                                   1)
                    v.wait_ge(sem["peb"], 2 * s + 2)
                    v.tensor_copy(out=rb[:], in_=p_bc[:]).then_inc(sem["dvbc"],
                                                                   1)
                    for hk in range(HK):
                        nx = 32 * s + 16 + hk
                        XS.wait(v, nx)
                        v.tensor_tensor(out=xnf[:], in0=xsl[nx % 3][:],
                                        in1=mb[:], op=ALU.subtract)
                        v.tensor_tensor(out=_sl(xns, hk, SEG + 1), in0=xnf[:],
                                        in1=rb[:],
                                        op=ALU.mult).then_inc(sem["dvn"], 1)
                    if s >= 3:
                        for hk in range(HK):
                            v.tensor_copy(
                                out=_sl(xnm, hk, T + 1, (s - 3) * SEG,
                                        (s - 3) * SEG + SEG + 1),
                                in_=_sl(xns, hk, SEG + 1))
                    def mix(tm, tm1):
                        last = None
                        for hk in range(HK):
                            v.tensor_scalar(out=tsc[:],
                                            in0=_sl(xns, hk, SEG + 1, 1,
                                                    SEG + 1),
                                            scalar1=tm[:, hk:hk + 1],
                                            scalar2=None, op0=ALU.mult)
                            last = v.scalar_tensor_tensor(
                                out=_sl(xm, hk, SEG),
                                in0=_sl(xns, hk, SEG + 1, 0, SEG),
                                scalar=tm1[:, hk:hk + 1], in1=tsc[:],
                                op0=ALU.mult, op1=ALU.add)
                        last.then_inc(sem["dvmx"], 1)
                    mix(tmk, tmk1)
                    CONV.emit_dve(v, 0 if s == 0 else 9)
                    mix(tmv, tmv1)
                    CONV.emit_dve(v, 0 if s == 0 else 3)
                    pend = None
                    for oc in range(OC):
                        gidx = 32 * s + 16 + oc
                        v.wait_ge(sem["pew"], gidx + 1)
                        v.wait_ge(sem["dvev"], 32 * s + oc + 1)
                        if s >= 3:
                            kvdst = _sl(kvm, oc, T, (s - 3) * SEG,
                                        (s - 3) * SEG + SEG)
                        else:
                            kvdst = _sl(kvm, oc, T, 0, SEG)
                        v.tensor_tensor(out=kvdst, in0=_sl(kbuf, oc, SEG),
                                        in1=p_kv[gidx % 2][:],
                                        op=ALU.mult).then_inc(sem["dvev"], 1)
                        v.wait_ge(sem["dvev"], gidx + 1)
                        if s == 3:
                            v.tensor_copy(out=carry3[:, oc:oc + 1],
                                          in_=carry[:, oc:oc + 1])
                        lb = lbf[oc % 2]
                        v.tensor_tensor_scan(
                            out=lb[:, 1:SEG + 1],
                            data0=wcol[:, oc:oc + 1].broadcast_to([128, SEG]),
                            data1=kvdst, initial=carry[:, oc:oc + 1],
                            op0=ALU.mult, op1=ALU.add).then_inc(sem["dvq"], 1)
                        C["dvq"] += 1
                        if s == 0 and oc < 2:
                            CONV.emit_dve(v, 1)
                        if pend is not None:
                            po, pt = pend
                            v.wait_ge(sem["dvq"], pt)
                            v.tensor_copy(out=carry[:, po:po + 1],
                                          in_=lbf[po % 2][:, SEG:SEG + 1])
                        pend = (oc, C["dvq"])
                    po, pt = pend
                    v.wait_ge(sem["dvq"], pt)
                    v.tensor_copy(out=carry[:, po:po + 1],
                                  in_=lbf[po % 2][:, SEG:SEG + 1])
                v.tensor_copy(out=carry3[:, 0:1],
                              in_=carry3[:, 0:1]).then_inc(sem["phb"], 1)

            @blk.gpsimd
            def _(g):
                bw(g)
                CONV.emit_gp(g, CONV.ng + 4)
                for n in range(64):
                    w, oc, h = n // 32, (n // 2) % 16, n % 2
                    gsl_ = (n // 2) % 3
                    g.wait_ge(sem["acc"], ACCB + n + 1)
                    SG.dma(g, oblk(cdst[w])[oc][:, 8 * h:8 * h + 8, :],
                           wsl[gsl_][:, 1024 * h:1024 * h + 1024
                                     ].rearrange("p (hk i) -> p hk i", hk=8),
                           n)
                CONV.emit_gp(g, CONV.nd + 1)
            C["dvst"] += NSEG
            C["dvbc"] += NSEG * 2
            C["dvmx"] += NSEG * 2
            C["dvev"] += NSEG * 32
            PHB[0] += 1

        if DEBUG:
            with ExitStack() as phd:
                blkd = phd.enter_context(nc.Block())

                @blkd.gpsimd
                def _(g):
                    bw(g)
                    g.dma_start(out=dbg_xnm, in_=xnm[:]).then_inc(sem["stg"], 16)
                    g.dma_start(out=dbg_kvm, in_=kvm).then_inc(sem["stg"], 16)
                    g.wait_ge(sem["stg"], C["stg"] + 32)
            C["stg"] += 32

        ACC3 = C["acc"]
        PEW3 = C["pew"]
        DVMX3 = C["dvmx"]
        with ExitStack() as ph:
            tsc3 = ph.enter_context(nc.sbuf_tensor("tsc3", [128, T], BF16))
            wf3 = [ph.enter_context(nc.sbuf_tensor(f"wf3{k}", [128, H], F32))
                   for k in range(2)]
            wb3 = [ph.enter_context(nc.sbuf_tensor(f"wb3{k}", [128, H], BF16))
                   for k in range(2)]
            rsg = [ph.enter_context(nc.sbuf_tensor(f"rsg{k}", [128, T], BF16))
                   for k in range(2)]
            t1s = ph.enter_context(nc.sbuf_tensor("t1s", [128, T], F32))
            lb3 = ph.enter_context(nc.sbuf_tensor("lb3", [128, T + 1], F32))
            p_r = [ph.enter_context(nc.psum_tensor(f"p_r{k}", [128, T], F32))
                   for k in range(2)]
            blk = ph.enter_context(nc.Block())

            @blk.sync
            def _(sync):
                bw(sync)
                for oc in range(OC):
                    if oc >= 2:
                        sync.wait_ge(sem["acc"], ACC3 + oc - 1)
                    FS.dma(sync,
                           wf3[oc % 2].ap().rearrange("p (hk i) -> p hk i",
                                                      hk=HK),
                           oblk(d_wr)[oc], oc)

            @blk.scalar
            def _(sc):
                bw(sc)
                def sig(oc):
                    sc.wait_ge(sem["pew"], PEW3 + oc + 1)
                    if oc >= 2:
                        sc.wait_ge(sem["dvat"], oc - 1)
                    sc.activation(out=rsg[oc % 2][:], in_=p_r[oc % 2][:],
                                  func=AF.Sigmoid).then_inc(sem["acs"], 1)
                for oc in range(OC):
                    if oc >= 2:
                        sig(oc - 2)
                        sc.wait_ge(sem["pew"], PEW3 + oc - 1)
                    FS.wait(sc, oc)
                    sc.activation(out=wb3[oc % 2][:], in_=wf3[oc % 2][:],
                                  func=AF.Identity).then_inc(sem["acc"], 1)
                sig(OC - 2)
                sig(OC - 1)
            C["acc"] += 16
            C["acs"] += 16

            @blk.tensor
            def _(t):
                bw(t)
                for oc in range(OC):
                    t.wait_ge(sem["acc"], ACC3 + oc + 1)
                    t.wait_ge(sem["dvmx"], DVMX3 + 1)
                    if oc >= 2:
                        t.wait_ge(sem["acs"], oc - 1)
                    for hk in range(HK):
                        lhsT = _sl(wb3[oc % 2], hk, 128)
                        t.matmul(out=p_r[oc % 2][:, 0:512], lhsT=lhsT,
                                 rhs=_sl(xmr, hk, T, 0, 512),
                                 start=(hk == 0), stop=(hk == HK - 1))
                        mm = t.matmul(out=p_r[oc % 2][:, 512:1024], lhsT=lhsT,
                                      rhs=_sl(xmr, hk, T, 512, 1024),
                                      start=(hk == 0), stop=(hk == HK - 1))
                    mm.then_inc(sem["pew"], 1)
            C["pew"] += 16

            @blk.vector
            def _(v):
                bw(v)
                last = None
                for hk in range(HK):
                    v.tensor_scalar(out=tsc3[:],
                                    in0=_sl(xnm, hk, T + 1, 1, T + 1),
                                    scalar1=tmr[:, hk:hk + 1], scalar2=None,
                                    op0=ALU.mult)
                    last = v.scalar_tensor_tensor(
                        out=_sl(xmr, hk, T), in0=_sl(xnm, hk, T + 1, 0, T),
                        scalar=tmr1[:, hk:hk + 1], in1=tsc3[:],
                        op0=ALU.mult, op1=ALU.add)
                last.then_inc(sem["dvmx"], 1)
                for oc in range(OC):
                    v.tensor_copy(out=lb3[:, 0:1], in_=carry3[:, oc:oc + 1])
                    v.tensor_tensor_scan(
                        out=lb3[:, 1:T + 1],
                        data0=wcol[:, oc:oc + 1].broadcast_to([128, T]),
                        data1=_sl(kvm, oc, T), initial=carry3[:, oc:oc + 1],
                        op0=ALU.mult, op1=ALU.add).then_inc(sem["dvq"], 1)
                    C["dvq"] += 1
                    v.wait_ge(sem["dvq"], C["dvq"])
                    v.scalar_tensor_tensor(out=t1s[:], in0=_sl(kvm, oc, T),
                                           scalar=ucol[:, oc:oc + 1],
                                           in1=lb3[:, 0:T],
                                           op0=ALU.mult, op1=ALU.add)
                    v.wait_ge(sem["acs"], oc + 1)
                    v.tensor_tensor(out=_sl(kvm, oc, T), in0=t1s[:],
                                    in1=rsg[oc % 2][:],
                                    op=ALU.mult).then_inc(sem["dvat"], 1)
                    CONV.emit_dve(v, 1)
                v.tensor_copy(out=carry3[:, 0:1],
                              in_=carry3[:, 0:1]).then_inc(sem["phb"], 1)

            @blk.gpsimd
            def _(g):
                bw(g)
                CONV.emit_gp(g, CONV.nd + 1)
            C["dvmx"] += 1
            C["dvat"] += 16
            PHB[0] += 1

        if DEBUG:
            with ExitStack() as phd:
                blkd = phd.enter_context(nc.Block())

                @blkd.gpsimd
                def _(g):
                    bw(g)
                    g.dma_start(out=dbg_attn, in_=kvm).then_inc(sem["stg"], 16)
                    g.wait_ge(sem["stg"], C["stg"] + 16)
            C["stg"] += 16

        ACC4 = C["acc"]
        PEW4 = C["pew"]
        DVH4 = C["dvh"]
        with ExitStack() as ph:
            wf4 = [ph.enter_context(nc.sbuf_tensor(f"wf4{k}", [128, H], F32))
                   for k in range(2)]
            wb4 = [ph.enter_context(nc.sbuf_tensor(f"wb4{k}", [128, H], BF16))
                   for k in range(2)]
            p_s = [ph.enter_context(nc.psum_tensor(f"p_s{k}", [128, T], F32))
                   for k in range(2)]
            blk = ph.enter_context(nc.Block())

            @blk.sync
            def _(sync):
                bw(sync)
                for hc in range(HK):
                    if hc >= 2:
                        sync.wait_ge(sem["acc"], ACC4 + hc - 1)
                    FS.dma(sync,
                           wf4[hc % 2].ap().rearrange("p (hk i) -> p hk i",
                                                      hk=HK),
                           oblk(d_wo)[hc], 16 + hc)

            @blk.scalar
            def _(sc):
                bw(sc)
                for hc in range(HK):
                    FS.wait(sc, 16 + hc)
                    if hc >= 2:
                        sc.wait_ge(sem["pew"], PEW4 + hc - 1)
                    sc.activation(out=wb4[hc % 2][:], in_=wf4[hc % 2][:],
                                  func=AF.Identity).then_inc(sem["acc"], 1)
            C["acc"] += 16

            @blk.tensor
            def _(t):
                bw(t)
                for hc in range(HK):
                    t.wait_ge(sem["acc"], ACC4 + hc + 1)
                    if hc >= 2:
                        t.wait_ge(sem["dvh"], DVH4 + hc - 1)
                    for oc in range(OC):
                        lhsT = _sl(wb4[hc % 2], oc, 128)
                        t.matmul(out=p_s[hc % 2][:, 0:512], lhsT=lhsT,
                                 rhs=_sl(kvm, oc, T, 0, 512),
                                 start=(oc == 0), stop=(oc == OC - 1))
                        mm = t.matmul(out=p_s[hc % 2][:, 512:1024], lhsT=lhsT,
                                      rhs=_sl(kvm, oc, T, 512, 1024),
                                      start=(oc == 0), stop=(oc == OC - 1))
                    mm.then_inc(sem["pew"], 1)
            C["pew"] += 16

            @blk.vector
            def _(v):
                bw(v)
                for hc in range(HK):
                    v.wait_ge(sem["pew"], PEW4 + hc + 1)
                    v.tensor_copy(out=_sl(sout, hc, T),
                                  in_=p_s[hc % 2][:]).then_inc(sem["dvh"], 1)
                    CONV.emit_dve(v, 1)
                v.tensor_copy(out=carry3[:, 0:1],
                              in_=carry3[:, 0:1]).then_inc(sem["phb"], 1)

            @blk.gpsimd
            def _(g):
                bw(g)
                CONV.emit_gp(g, CONV.nd + 1)
            C["dvh"] += 16
            PHB[0] += 1

        PEWB = C["pew"]
        ACSB = C["acs"]
        DVHB = C["dvh"]

        def pew_gu(th, ic, w):
            return PEWB + th * 102 + 2 * ic + w + 1

        def pew_d(th, hc):
            return PEWB + th * 102 + 86 + hc + 1

        def acs_gu(th, ic):
            return ACSB + th * 59 + ic + 1

        def acs_d(th, hc):
            return ACSB + th * 59 + 43 + hc + 1

        def dvh_gu(th, ic):
            return DVHB + th * 59 + ic + 1

        def dvh_d(th, hc):
            return DVHB + th * 59 + 43 + hc + 1

        with ExitStack() as ph:
            gsl = [ph.enter_context(nc.sbuf_tensor(f"gsl{k}", [128, H], BF16))
                   for k in range(2)]
            usl = [ph.enter_context(nc.sbuf_tensor(f"usl{k}", [128, H], BF16))
                   for k in range(2)]
            hgs = [ph.enter_context(nc.sbuf_tensor(f"hgs{k}", [128, SEG], F32))
                   for k in range(2)]
            dsl = [ph.enter_context(
                nc.sbuf_tensor(f"dsl{k}", [128, IC * 128], BF16))
                for k in range(2)]
            xs6 = [ph.enter_context(nc.sbuf_tensor(f"xs6{k}", [128, SEG], F32))
                   for k in range(2)]
            gs6 = [ph.enter_context(nc.sbuf_tensor(f"gs6{k}", [128, SEG], F32))
                   for k in range(2)]
            ob6 = [ph.enter_context(nc.sbuf_tensor(f"ob6{k}", [128, SEG], F32))
                   for k in range(2)]
            ts6 = ph.enter_context(nc.sbuf_tensor("ts6", [128, SEG], F32))
            p_g = [ph.enter_context(nc.psum_tensor(f"p_g{k}", [128, SEG], F32))
                   for k in range(2)]
            p_u = [ph.enter_context(nc.psum_tensor(f"p_u{k}", [128, SEG], F32))
                   for k in range(2)]
            p_d = [ph.enter_context(nc.psum_tensor(f"p_d{k}", [128, SEG], F32))
                   for k in range(2)]
            blk = ph.enter_context(nc.Block())

            @blk.sync
            def _(sync):
                bw(sync)
                for th in range(2):
                    for ic in range(IC):
                        gi = th * IC + ic
                        if gi >= 2:
                            t2, i2 = divmod(gi - 2, IC)
                            sync.wait_ge(sem["pew"], pew_gu(t2, i2, 1))
                        L = CONV.last[("u", ic)]
                        CVS.wait(sync, L // 2)
                        spin(sync, L // 2 + 1)
                        GS.dma(sync,
                               gsl[gi % 2].ap().rearrange(
                                   "p (hk i) -> p hk i", hk=HK),
                               wsv[:, 16 * ic:16 * ic + 16, :], gi)
                        US.dma(sync,
                               usl[gi % 2].ap().rearrange(
                                   "p (hk i) -> p hk i", hk=HK),
                               wsv[:, 688 + 16 * ic:688 + 16 * ic + 16, :], gi)
                    for hc in range(HK):
                        gj = th * HK + hc
                        if gj >= 2:
                            t2, h2 = divmod(gj - 2, HK)
                            sync.wait_ge(sem["pew"], pew_d(t2, h2))
                            sync.wait_ge(sem["dvh"], dvh_d(t2, h2))
                        L = CONV.last[("d", hc)]
                        CVS.wait(sync, L // 2)
                        spin(sync, L // 2 + 1)
                        DSm.dma(sync,
                                dsl[gj % 2].ap().rearrange(
                                    "p (ic j) -> p ic j", ic=IC),
                                wsv[:, 1376 + 43 * hc:1376 + 43 * hc + 43, :],
                                gj)
                        YS.dma(sync, xs6[gj % 2][:],
                               xr[hc, :, W + 1 + SEG * th:
                                  W + 1 + SEG * (th + 1)], gj)

            @blk.tensor
            def _(t):
                bw(t)
                for th in range(2):
                    TH0 = 1 + SEG * th
                    for ic in range(IC):
                        gi = th * IC + ic
                        GS.wait(t, gi)
                        if gi >= 2:
                            t2, i2 = divmod(gi - 2, IC)
                            t.wait_ge(sem["acs"], acs_gu(t2, i2))
                        for hk in range(HK):
                            mm = t.matmul(out=p_g[gi % 2][:],
                                          lhsT=_sl(gsl[gi % 2], hk, 128),
                                          rhs=_sl(xnm, hk, T + 1, TH0,
                                                  TH0 + SEG),
                                          start=(hk == 0), stop=(hk == HK - 1))
                        mm.then_inc(sem["pew"], 1)
                        US.wait(t, gi)
                        if gi >= 2:
                            t2, i2 = divmod(gi - 2, IC)
                            t.wait_ge(sem["dvh"], dvh_gu(t2, i2))
                        for hk in range(HK):
                            mm = t.matmul(out=p_u[gi % 2][:],
                                          lhsT=_sl(usl[gi % 2], hk, 128),
                                          rhs=_sl(xnm, hk, T + 1, TH0,
                                                  TH0 + SEG),
                                          start=(hk == 0), stop=(hk == HK - 1))
                        mm.then_inc(sem["pew"], 1)
                    t.wait_ge(sem["dvh"], dvh_gu(th, IC - 1))
                    for hc in range(HK):
                        gj = th * HK + hc
                        DSm.wait(t, gj)
                        if gj >= 2:
                            t2, h2 = divmod(gj - 2, HK)
                            t.wait_ge(sem["acs"], acs_d(t2, h2))
                        for ic in range(IC):
                            mm = t.matmul(out=p_d[gj % 2][:],
                                          lhsT=_sl(dsl[gj % 2], ic, 128),
                                          rhs=_sl(hact, ic, SEG),
                                          start=(ic == 0), stop=(ic == IC - 1))
                        mm.then_inc(sem["pew"], 1)
            C["pew"] += 2 * (2 * IC + HK)

            @blk.scalar
            def _(sc):
                bw(sc)
                for th in range(2):
                    for ic in range(IC):
                        gi = th * IC + ic
                        sc.wait_ge(sem["pew"], pew_gu(th, ic, 0))
                        if gi >= 2:
                            t2, i2 = divmod(gi - 2, IC)
                            sc.wait_ge(sem["dvh"], dvh_gu(t2, i2))
                        sc.activation(out=hgs[gi % 2][:], in_=p_g[gi % 2][:],
                                      func=AF.Silu).then_inc(sem["acs"], 1)
                    for hc in range(HK):
                        gj = th * HK + hc
                        sc.wait_ge(sem["pew"], pew_d(th, hc))
                        if gj >= 2:
                            t2, h2 = divmod(gj - 2, HK)
                            sc.wait_ge(sem["dvh"], dvh_d(t2, h2))
                        sc.activation(out=gs6[gj % 2][:], in_=p_d[gj % 2][:],
                                      func=AF.Silu).then_inc(sem["acs"], 1)
            C["acs"] += 2 * (IC + HK)

            @blk.vector
            def _(v):
                bw(v)
                for th in range(2):
                    if th == 1:
                        v.wait_ge(sem["pew"], pew_d(0, HK - 1))
                    for ic in range(IC):
                        gi = th * IC + ic
                        CONV.emit_dve(v, 4)
                        v.wait_ge(sem["acs"], acs_gu(th, ic))
                        v.wait_ge(sem["pew"], pew_gu(th, ic, 1))
                        v.tensor_tensor(out=_sl(hact, ic, SEG),
                                        in0=hgs[gi % 2][:],
                                        in1=p_u[gi % 2][:],
                                        op=ALU.mult).then_inc(sem["dvh"], 1)
                    for hc in range(HK):
                        gj = th * HK + hc
                        v.wait_ge(sem["acs"], acs_d(th, hc))
                        YS.wait(v, gj)
                        if gj >= 2:
                            OS.wait(v, gj - 2)
                        v.tensor_tensor(out=ts6[:], in0=gs6[gj % 2][:],
                                        in1=_sl(sout, hc, T, SEG * th,
                                                SEG * (th + 1)),
                                        op=ALU.mult)
                        v.tensor_tensor(out=ob6[gj % 2][:], in0=ts6[:],
                                        in1=xs6[gj % 2][:],
                                        op=ALU.add).then_inc(sem["dvh"], 1)
            C["dvh"] += 2 * (IC + HK)

            @blk.gpsimd
            def _(g):
                bw(g)
                CONV.emit_gp(g, CONV.nd + 1)
                if CONV.nd >= NTL and not CONV.flushed:
                    CONV.flushed = True
                    CONV.flush_gp(g)
                for th in range(2):
                    for hc in range(HK):
                        gj = th * HK + hc
                        g.wait_ge(sem["dvh"], dvh_d(th, hc))
                        OS.dma(g, outr[hc][:, SEG * th:SEG * (th + 1)],
                               ob6[gj % 2][:], gj)
                OS.wait_all(g, 2 * HK)
            PHB[0] += 0
        if DEBUG:
            with ExitStack() as phd:
                blkd = phd.enter_context(nc.Block())

                @blkd.gpsimd
                def _(g):
                    bw(g)
                    g.dma_start(out=dbg_s, in_=sout).then_inc(sem["stg"], 16)
                    g.wait_ge(sem["stg"], C["stg"] + 16)
            C["stg"] += 16
    return nc


def kernel(L0_attn_hidden_states, L0_attn_Wr, L0_attn_Wk, L0_attn_Wv,
           L0_attn_Wo, L0_attn_time_decay, L0_attn_time_first,
           L0_attn_time_mix_k, L0_attn_time_mix_v, L0_attn_time_mix_r,
           L0_ffn_lut, L0_ffn_gate_idx, L0_ffn_up_idx, L0_ffn_down_idx):
    x = np.asarray(L0_attn_hidden_states, dtype=np.float32)
    Wr = np.asarray(L0_attn_Wr, dtype=np.float32)
    Wk = np.asarray(L0_attn_Wk, dtype=np.float32)
    Wv = np.asarray(L0_attn_Wv, dtype=np.float32)
    Wo = np.asarray(L0_attn_Wo, dtype=np.float32)
    td = np.asarray(L0_attn_time_decay, dtype=np.float32)
    tf = np.asarray(L0_attn_time_first, dtype=np.float32)
    tmk = np.asarray(L0_attn_time_mix_k, dtype=np.float32).reshape(-1)
    tmv = np.asarray(L0_attn_time_mix_v, dtype=np.float32).reshape(-1)
    tmr = np.asarray(L0_attn_time_mix_r, dtype=np.float32).reshape(-1)
    lut = np.asarray(L0_ffn_lut, dtype=np.float32).reshape(-1)
    gidx = np.asarray(L0_ffn_gate_idx, dtype=np.int32)
    uidx = np.asarray(L0_ffn_up_idx, dtype=np.int32)
    didx = np.asarray(L0_ffn_down_idx, dtype=np.int32)

    gidxT = np.ascontiguousarray(gidx.T)
    uidxT = np.ascontiguousarray(uidx.T)
    didxT = np.ascontiguousarray(didx.T)
    gidx_blk = np.ascontiguousarray(
        gidxT.reshape(H, IC, 128).transpose(1, 0, 2))
    uidx_blk = np.ascontiguousarray(
        uidxT.reshape(H, IC, 128).transpose(1, 0, 2))
    didx_blk = np.ascontiguousarray(
        didxT.reshape(I5504, HK, 128).transpose(1, 0, 2))

    allt = np.empty((NT, 128, 1024), np.int32)
    for ic in range(IC):
        for r in range(4):
            fam = gidx_blk if r < 2 else uidx_blk
            h2 = r % 2
            blkv = fam[ic].reshape(16, 128, 128)[8 * h2:8 * h2 + 8]
            allt[4 * ic + r] = blkv.transpose(1, 0, 2).reshape(128, 1024)
    dflat = didx_blk.reshape(HK * IC, 128, 128)
    for m in range(NT - 172):
        allt[172 + m] = dflat[8 * m:8 * m + 8].transpose(1, 0, 2
                                                         ).reshape(128, 1024)
    cidx_par = [np.ascontiguousarray(allt[q::2].astype(np.int8))
                for q in range(2)]
    ptab = (BM + np.arange(NTL + 2, dtype=np.int32)).reshape(1, -1)

    def blkT(Wm):
        return np.ascontiguousarray(Wm.reshape(16, 128, H).transpose(0, 2, 1))

    wk_blk = blkT(Wk)
    wv_blk = blkT(Wv)
    wr_blk = blkT(Wr)
    wo_blk = blkT(Wo)
    tmk_c = np.ascontiguousarray(tmk.reshape(16, 128).T)
    tmv_c = np.ascontiguousarray(tmv.reshape(16, 128).T)
    tmr_c = np.ascontiguousarray(tmr.reshape(16, 128).T)
    u_c = np.ascontiguousarray(tf.reshape(16, 128).T)
    td_c = np.ascontiguousarray(td.reshape(16, 128).T)
    ones = np.ones((128, 128), np.float32)

    in2 = []
    for c in range(8):
        b, p = divmod(c, 4)
        xt = x[b].T
        xw = np.zeros((H, COLS), np.float32)
        t0 = T * p - (W + 1)
        lo = max(0, -t0)
        xw[:, lo:] = xt[:, t0 + lo:t0 + COLS]
        in2.append({
            "x_fm": np.ascontiguousarray(xw),
            "wk_blk": wk_blk, "wv_blk": wv_blk, "wr_blk": wr_blk,
            "wo_blk": wo_blk, "cidx": cidx_par[c & 1],
            "pmeta": np.array([[c & 1, 0, 0, 0]], np.int32), "ptab": ptab,
            "tmk": tmk_c, "tmv": tmv_c, "tmr": tmr_c,
            "ucol": u_c, "tdcol": td_c, "ones": ones,
        })
    nc2 = build_main_nc(lut)
    res2 = run_bass_kernel_spmd(nc2, in2, core_ids=list(range(8)))

    out = np.empty((2, 4096, H), np.float32)
    for c in range(8):
        b, p = divmod(c, 4)
        out[b, T * p:T * (p + 1), :] = res2.results[c]["out_fm"].T
    return out


# version 23
# speedup vs baseline: 1.0412x; 1.0412x over previous
import numpy as np
import ml_dtypes
import concourse.bass as bass
import concourse.mybir as mybir
from concourse.bass_utils import run_bass_kernel_spmd
from contextlib import ExitStack

F32 = mybir.dt.float32
BF16 = mybir.dt.bfloat16
I32 = mybir.dt.int32
I8 = mybir.dt.int8
AF = mybir.ActivationFunctionType
ALU = mybir.AluOpType
bfloat16 = ml_dtypes.bfloat16

H = 2048
I5504 = 5504
HK = 16
OC = 16
IC = 43
T = 1024
W = 1536
COLS = T + W + 1
NSEG = 5
SEG = 512
EPS = 1e-5
NT = 258
NTL = NT // 2
BM = 0x5A170000


def _blk_base(t):
    if t < 172:
        ic, r = divmod(t, 4)
        return (0 if r < 2 else 688) + 16 * ic + 8 * (r % 2)
    return 1376 + 8 * (t - 172)


def _sl(buf, k, w, a=0, b=None):
    if b is None:
        b = w
    return buf[:, k * w + a:k * w + b]


def build_conv_nc(lut):
    nc = bass.Bass("TRN2", target_bir_lowering=False, detect_race_conditions=False)
    d_g = nc.dram_tensor("g1_gidx", [IC, 256, 128], I32, kind="ExternalInput").ap()
    d_u = nc.dram_tensor("g1_uidx", [IC, 256, 128], I32, kind="ExternalInput").ap()
    d_d = nc.dram_tensor("g1_didx", [2, I5504, 128], I32, kind="ExternalInput").ap()
    o_g = nc.dram_tensor("g1_wg", [IC, 256, 128], BF16, kind="ExternalOutput").ap()
    o_u = nc.dram_tensor("g1_wu", [IC, 256, 128], BF16, kind="ExternalOutput").ap()
    o_d = nc.dram_tensor("g1_wd", [2, I5504, 128], BF16, kind="ExternalOutput").ap()
    NF = IC * 2 * 128
    with ExitStack() as ctx:
        s_i = [ctx.enter_context(nc.sbuf_tensor(f"s_i{k}", [128, NF], I32))
               for k in range(2)]
        s_b = ctx.enter_context(nc.sbuf_tensor("s_b", [128, NF], BF16))
        s_m = ctx.enter_context(nc.sbuf_tensor("s_m", [128, NF], BF16))
        s_a = [ctx.enter_context(nc.sbuf_tensor(f"s_a{k}", [128, NF], BF16))
               for k in range(2)]
        ld = [ctx.enter_context(nc.semaphore(f"ld{k}")) for k in range(2)]
        dv = ctx.enter_context(nc.semaphore("dv"))
        st = [ctx.enter_context(nc.semaphore(f"st{k}")) for k in range(2)]
        block = ctx.enter_context(nc.Block())

        def view4(ap_, a, b):
            return ap_.rearrange("p (x y i) -> p x y i", x=a, y=b)

        srcs = [d_g.rearrange("ic (rr p) i -> p ic rr i", p=128),
                d_u.rearrange("ic (rr p) i -> p ic rr i", p=128),
                d_d.rearrange("b (jj p) i -> p b jj i", p=128)]
        dsts = [o_g.rearrange("ic (rr p) i -> p ic rr i", p=128),
                o_u.rearrange("ic (rr p) i -> p ic rr i", p=128),
                o_d.rearrange("b (jj p) i -> p b jj i", p=128)]
        shp = [(IC, 2), (IC, 2), (2, 43)]

        @block.sync
        def _(sync):
            for n in range(3):
                if n >= 2:
                    sync.wait_ge(dv, 2 * (n - 2) + 1)
                sync.dma_start(out=view4(s_i[n % 2].ap(), *shp[n]),
                               in_=srcs[n]).then_inc(ld[n % 2], 16)

        @block.vector
        def _(v):
            for n in range(3):
                v.wait_ge(ld[n % 2], 16 * (n // 2 + 1))
                v.tensor_copy(out=s_b[:], in_=s_i[n % 2][:]).then_inc(dv, 1)
                if n >= 2:
                    v.wait_ge(st[n % 2], 16)
                acc = s_a[n % 2]
                last = None
                for j in range(16):
                    if j == 0:
                        v.tensor_scalar(out=acc[:], in0=s_b[:], scalar1=0.0,
                                        scalar2=float(lut[0]),
                                        op0=ALU.is_equal, op1=ALU.mult)
                    else:
                        v.tensor_scalar(out=s_m[:], in0=s_b[:], scalar1=float(j),
                                        scalar2=float(lut[j]),
                                        op0=ALU.is_equal, op1=ALU.mult)
                        last = v.tensor_tensor(out=acc[:], in0=acc[:],
                                               in1=s_m[:], op=ALU.add)
                last.then_inc(dv, 1)

        @block.gpsimd
        def _(g):
            for n in range(3):
                g.wait_ge(dv, 2 * (n + 1))
                g.dma_start(out=dsts[n],
                            in_=view4(s_a[n % 2].ap(), *shp[n])
                            ).then_inc(st[n % 2], 16)
            g.wait_ge(st[0], 32)
            g.wait_ge(st[1], 16)
    return nc


DEBUG = False


def build_main_nc(lut):
    nc = bass.Bass("TRN2", target_bir_lowering=False, detect_race_conditions=False)
    d_x = nc.dram_tensor("x_fm", [H, COLS], F32, kind="ExternalInput").ap()
    d_wk = nc.dram_tensor("wk_blk", [OC, H, 128], F32, kind="ExternalInput").ap()
    d_wv = nc.dram_tensor("wv_blk", [OC, H, 128], F32, kind="ExternalInput").ap()
    d_wr = nc.dram_tensor("wr_blk", [OC, H, 128], F32, kind="ExternalInput").ap()
    d_wo = nc.dram_tensor("wo_blk", [OC, H, 128], F32, kind="ExternalInput").ap()
    d_ci = nc.dram_tensor("cidx", [NTL, 128, 1024], I8,
                          kind="ExternalInput").ap()
    d_pm = nc.dram_tensor("pmeta", [1, 4], I32, kind="ExternalInput").ap()
    d_pt = nc.dram_tensor("ptab", [1, NTL + 2], I32,
                          kind="ExternalInput").ap()
    d_ws = nc.dram_tensor("wscr", [NT * 8, 128, 128], BF16,
                          kind="Internal", addr_space="Shared").ap()
    sprog = nc.dram_tensor("sprog", [1, 2], I32, kind="Internal",
                           addr_space="Shared").ap()
    d_tmk = nc.dram_tensor("tmk", [128, 16], F32, kind="ExternalInput").ap()
    d_tmv = nc.dram_tensor("tmv", [128, 16], F32, kind="ExternalInput").ap()
    d_tmr = nc.dram_tensor("tmr", [128, 16], F32, kind="ExternalInput").ap()
    d_u = nc.dram_tensor("ucol", [128, 16], F32, kind="ExternalInput").ap()
    d_td = nc.dram_tensor("tdcol", [128, 16], F32, kind="ExternalInput").ap()
    d_ones = nc.dram_tensor("ones", [128, 128], F32, kind="ExternalInput").ap()
    d_out = nc.dram_tensor("out_fm", [H, T], F32, kind="ExternalOutput").ap()
    c_wk = nc.dram_tensor("wkbf", [OC, H, 128], BF16, kind="Internal").ap()
    c_wv = nc.dram_tensor("wvbf", [OC, H, 128], BF16, kind="Internal").ap()
    if DEBUG:
        dbg_xnm = nc.dram_tensor("dbg_xnm", [128, HK * (T + 1)], BF16,
                                 kind="ExternalOutput").ap()
        dbg_kvm = nc.dram_tensor("dbg_kvm", [128, OC * T], BF16,
                                 kind="ExternalOutput").ap()
        dbg_attn = nc.dram_tensor("dbg_attn", [128, OC * T], BF16,
                                  kind="ExternalOutput").ap()
        dbg_s = nc.dram_tensor("dbg_s", [128, HK * T], BF16,
                               kind="ExternalOutput").ap()
        dbg_hact = nc.dram_tensor("dbg_hact", [128, IC * SEG], BF16,
                                  kind="ExternalOutput").ap()

    xr = d_x.rearrange("(hk p) c -> hk p c", p=128)
    outr = d_out.rearrange("(hk p) c -> hk p c", p=128)

    def oblk(d):
        return d.rearrange("o (hk p) i -> o p hk i", p=128)

    wsv = d_ws.rearrange("b p j -> p b j")

    with ExitStack() as outer:
        arena = outer.enter_context(nc.sbuf_tensor("arena", [128, 49152], BF16))
        kvm = arena[:, 0:16384]
        kbuf = arena[:, 16384:32768]
        xm = arena[:, 32768:49152]
        xmr = arena[:, 16384:32768]
        sout = arena[:, 32768:49152]
        hact = arena[:, 0:22016]
        xnm = outer.enter_context(nc.sbuf_tensor("xnm", [128, HK * (T + 1)],
                                                 BF16))
        carry = outer.enter_context(nc.sbuf_tensor("carry", [128, 16], F32))
        carry3 = outer.enter_context(nc.sbuf_tensor("carry3", [128, 16], F32))
        cons = outer.enter_context(nc.sbuf_tensor("cons", [128, 128], F32))
        tmk = outer.enter_context(nc.sbuf_tensor("tmk_s", [128, 16], F32))
        tmv = outer.enter_context(nc.sbuf_tensor("tmv_s", [128, 16], F32))
        tmr = outer.enter_context(nc.sbuf_tensor("tmr_s", [128, 16], F32))
        tmk1 = outer.enter_context(nc.sbuf_tensor("tmk1", [128, 16], F32))
        tmv1 = outer.enter_context(nc.sbuf_tensor("tmv1", [128, 16], F32))
        tmr1 = outer.enter_context(nc.sbuf_tensor("tmr1", [128, 16], F32))
        ucol = outer.enter_context(nc.sbuf_tensor("ucol_s", [128, 16], F32))
        tdc = outer.enter_context(nc.sbuf_tensor("tdc", [128, 16], F32))
        edc = outer.enter_context(nc.sbuf_tensor("edc", [128, 16], F32))
        wcol = outer.enter_context(nc.sbuf_tensor("wcol", [128, 16], F32))

        sem = {}
        for name in ("ldx", "ldw", "pex", "pew", "peb", "dvst", "dvbc", "dvmx",
                     "dvev", "dvat", "acx", "acst", "acc", "acs", "dvh", "st",
                     "stg", "phb", "dvn", "dvq", "pub", "pol"):
            sem[name] = outer.enter_context(nc.semaphore(name))
        C = {k: 0 for k in sem}
        PHB = [0]

        class DS:

            def __init__(self, name, nbuf):
                self.s = [outer.enter_context(nc.semaphore(f"{name}{k}"))
                          for k in range(nbuf)]
                self.nb = nbuf

            def dma(self, q, out, in_, idx):
                return q.dma_start(out=out, in_=in_).then_inc(
                    self.s[idx % self.nb], 16)

            def wait(self, eng, idx):
                eng.wait_ge(self.s[idx % self.nb], 16 * (idx // self.nb + 1))

            def wait_all(self, eng, total):
                for k in range(self.nb):
                    n = total // self.nb + (1 if k < total % self.nb else 0)
                    eng.wait_ge(self.s[k], 16 * n)

        NCVI = 3
        NCVA = 3
        CVL = DS("cvl", NCVI)
        CVS = DS("cvs", NCVA)
        cvc = outer.enter_context(nc.semaphore("cvc"))
        cvi = [outer.enter_context(nc.sbuf_tensor(f"cvi{k}", [128, 1024], I8))
               for k in range(NCVI)]
        cvb = outer.enter_context(nc.sbuf_tensor("cvb", [128, 1024], BF16))
        cvm = outer.enter_context(nc.sbuf_tensor("cvm", [128, 1024], BF16))
        cva = [outer.enter_context(nc.sbuf_tensor(f"cva{k}", [128, 1024], BF16))
               for k in range(NCVA)]
        s_pm = outer.enter_context(nc.sbuf_tensor("s_pm", [1, 4], I32))
        s_pt = outer.enter_context(nc.sbuf_tensor("s_pt", [1, NTL + 2], I32))
        s_pp = outer.enter_context(nc.sbuf_tensor("s_pp", [1, 2], I32))
        SPR = {"r": None}

        def spin(sync, cnt):
            if SPR["r"] is None:
                ra = sync.alloc_register("spa")
                rb = sync.alloc_register("spb")
                rc = sync.alloc_register("spc")
                sync.dma_start(out=s_pp[:], in_=d_pm[0:1, 1:3]
                               ).then_inc(sem["pol"], 16)
                sync.wait_ge(sem["pol"], 16)
                sync.reg_mov(ra, 0)
                sync.reg_mov(rb, 0)
                SPR["r"] = (ra, rb, rc)
            ra, rb, rc = SPR["r"]
            tgt = BM + cnt

            def check():
                sync.reg_alu(rc, ra, rb, ALU.min)
                sync.reg_alu(rc, rc, tgt, ALU.is_ge)
                sync.reg_alu(rc, rc, 1, ALU.bitwise_xor)
            check()
            with sync.While(rc):
                sync.dma_start(out=s_pp[:], in_=sprog).then_inc(sem["pol"], 16)
                sync.reg_load(ra, s_pp[0:1, 0:1])
                sync.reg_load(rb, s_pp[0:1, 1:2])
                check()

        class Converter:

            def __init__(self):
                self.last = {}
                for ic in range(IC):
                    self.last[("g", ic)] = 4 * ic + 1
                    self.last[("u", ic)] = 4 * ic + 3
                for hc in range(HK):
                    self.last[("d", hc)] = 172 + (43 * hc + 42) // 8
                self.nd = 0
                self.ng = 0
                self.flushed = False
                self.rpar = None

            def _setup_gp(self, g):
                if self.rpar is not None:
                    return
                g.dma_start(out=s_pm[:], in_=d_pm).then_inc(sem["pub"], 16)
                g.dma_start(out=s_pt[:], in_=d_pt).then_inc(sem["pub"], 16)
                g.wait_ge(sem["pub"], 32)
                self.rpar = g.alloc_register("rpar")
                g.reg_load(self.rpar, s_pm[0:1, 0:1])

            def emit_dve(self, v, k):
                for _ in range(k):
                    j = self.nd
                    if j >= NTL:
                        return
                    CVL.wait(v, j)
                    if j >= NCVA:
                        CVS.wait(v, j - NCVA)
                    acc = cva[j % NCVA]
                    v.tensor_copy(out=cvb[:], in_=cvi[j % NCVI][:])
                    lastop = None
                    for i in range(16):
                        if i == 0:
                            v.tensor_scalar(out=acc[:], in0=cvb[:],
                                            scalar1=0.0, scalar2=float(lut[0]),
                                            op0=ALU.is_equal, op1=ALU.mult)
                        else:
                            v.tensor_scalar(out=cvm[:], in0=cvb[:],
                                            scalar1=float(i),
                                            scalar2=float(lut[i]),
                                            op0=ALU.is_equal, op1=ALU.mult)
                            lastop = v.tensor_tensor(out=acc[:], in0=acc[:],
                                                     in1=cvm[:], op=ALU.add)
                    lastop.then_inc(cvc, 1)
                    self.nd += 1

            def _store(self, g, js):
                g.wait_ge(cvc, js + 1)
                src = cva[js % NCVA].ap().rearrange("p (a i) -> p a i", a=8)
                b0 = _blk_base(2 * js)
                b1 = _blk_base(2 * js + 1)
                with g.If_eq(self.rpar, 0):
                    CVS.dma(g, wsv[:, b0:b0 + 8, :], src, js)
                with g.Else():
                    CVS.dma(g, wsv[:, b1:b1 + 8, :], src, js)
                CVS.wait(g, js)
                with g.If_eq(self.rpar, 0):
                    g.dma_start(out=sprog[0:1, 0:1],
                                in_=s_pt[0:1, js + 1:js + 2]
                                ).then_inc(sem["pub"], 16)
                with g.Else():
                    g.dma_start(out=sprog[0:1, 1:2],
                                in_=s_pt[0:1, js + 1:js + 2]
                                ).then_inc(sem["pub"], 16)

            def emit_gp(self, g, upto):
                self._setup_gp(g)
                while self.ng < min(upto, NTL):
                    j = self.ng
                    if j >= NCVI - 1:
                        g.wait_ge(cvc, j - (NCVI - 1))
                    CVL.dma(g, cvi[j % NCVI][:], d_ci[j], j)
                    if j >= NCVA - 1:
                        self._store(g, j - (NCVA - 1))
                    self.ng += 1

            def flush_gp(self, g):
                for js in range(NTL - (NCVA - 1), NTL):
                    self._store(g, js)

        CONV = Converter()

        XS = DS("xs", 3)
        WS = DS("ws", 3)
        BS = DS("bs", 2)
        SG = DS("sg2", 2)
        FS = DS("fs", 2)
        GS = DS("gs", 2)
        US = DS("us", 2)
        DSm = DS("ds", 2)
        YS = DS("ys", 2)
        OS = DS("os", 2)

        def bw(eng):
            if PHB[0]:
                eng.wait_ge(sem["phb"], PHB[0])

        with ExitStack() as ph:
            blk = ph.enter_context(nc.Block())

            @blk.sync
            def _(sync):
                for d, s in ((d_ones, cons), (d_tmk, tmk), (d_tmv, tmv),
                             (d_tmr, tmr), (d_u, ucol), (d_td, tdc)):
                    sync.dma_start(out=s[:], in_=d).then_inc(sem["ldx"], 16)
            C["ldx"] += 96

            @blk.scalar
            def _(sc):
                sc.wait_ge(sem["ldx"], 96)
                sc.activation(out=edc[:], in_=tdc[:], func=AF.Exp)
                sc.activation(out=wcol[:], in_=edc[:], func=AF.Exp,
                              scale=-1.0).then_inc(sem["phb"], 1)

            @blk.vector
            def _(v):
                v.wait_ge(sem["ldx"], 96)
                v.memset(carry[:], 0.0)
                last = None
                for a, b in ((tmk, tmk1), (tmv, tmv1), (tmr, tmr1)):
                    last = v.tensor_scalar(out=b[:], in0=a[:], scalar1=-1.0,
                                           scalar2=1.0, op0=ALU.mult,
                                           op1=ALU.add)
                CONV.emit_dve(v, 16)
                last.then_inc(sem["phb"], 1)

            @blk.gpsimd
            def _(g):
                CONV.emit_gp(g, CONV.nd + 1)
            PHB[0] += 2

        ACCB = C["acc"]
        with ExitStack() as ph:
            c0f = [ph.enter_context(nc.sbuf_tensor(f"c0f{k}", [128, 1024],
                                                   F32)) for k in range(2)]
            csrc = [d_wk, d_wv]
            cdst = [c_wk, c_wv]
            xsl = [ph.enter_context(nc.sbuf_tensor(f"xsl{k}", [128, SEG + 1],
                                                   F32)) for k in range(3)]
            sqs = [ph.enter_context(nc.sbuf_tensor(f"sqs{k}", [128, SEG + 1],
                                                   F32)) for k in range(2)]
            xns = ph.enter_context(nc.sbuf_tensor("xns", [128, HK * (SEG + 1)],
                                                  BF16))
            tsc = ph.enter_context(nc.sbuf_tensor("tsc", [128, SEG], BF16))
            wsl = [ph.enter_context(nc.sbuf_tensor(f"wsl{k}", [128, H], BF16))
                   for k in range(3)]
            lbf = [ph.enter_context(nc.sbuf_tensor(f"lbf{k}", [128, SEG + 1],
                                                   F32)) for k in range(2)]
            xnf = ph.enter_context(nc.sbuf_tensor("xnf", [128, SEG + 1], F32))
            mrow = ph.enter_context(nc.sbuf_tensor("mrow", [1, SEG + 1], F32))
            vrow = ph.enter_context(nc.sbuf_tensor("vrow", [1, SEG + 1], F32))
            rrow = ph.enter_context(nc.sbuf_tensor("rrow", [1, SEG + 1], F32))
            mb = ph.enter_context(nc.sbuf_tensor("mb", [128, SEG + 1], F32))
            rb = ph.enter_context(nc.sbuf_tensor("rb", [128, SEG + 1], F32))
            p_sx = ph.enter_context(nc.psum_tensor("p_sx", [1, SEG + 1], F32))
            p_sq = ph.enter_context(nc.psum_tensor("p_sq", [1, SEG + 1], F32))
            p_bc = ph.enter_context(nc.psum_tensor("p_bc", [128, SEG + 1], F32))
            p_kv = [ph.enter_context(nc.psum_tensor(f"p_kv{k}", [128, SEG],
                                                    F32)) for k in range(2)]
            blk = ph.enter_context(nc.Block())

            @blk.sync
            def _(sync):
                bw(sync)
                for s in range(NSEG):
                    for j in range(32):
                        hk = j % 16
                        nx = 32 * s + j
                        if nx >= 3:
                            sp, jp = divmod(nx - 3, 32)
                            if jp < 16:
                                sync.wait_ge(sem["acx"], 16 * sp + jp + 1)
                                sync.wait_ge(sem["pex"], 16 * sp + jp + 1)
                            else:
                                sync.wait_ge(sem["dvn"], 16 * sp + jp - 16 + 1)
                        XS.dma(sync, xsl[nx % 3][:],
                               xr[hk, :, SEG * s:SEG * s + SEG + 1], nx)
                    if s == 0:
                        for n in range(64):
                            w, oc, h = n // 32, (n // 2) % 16, n % 2
                            if n >= 2:
                                sync.wait_ge(sem["acc"], ACCB + n - 1)
                            BS.dma(sync,
                                   c0f[n % 2].ap().rearrange(
                                       "p (hk i) -> p hk i", hk=8),
                                   oblk(csrc[w])[oc][:, 8 * h:8 * h + 8, :],
                                   n)
                    for w, cw in ((0, c_wk), (1, c_wv)):
                        for oc in range(OC):
                            if s == 0:
                                continue
                            gidx = 32 * s + 16 * w + oc
                            sync.wait_ge(sem["pew"], gidx - 2)
                            if s == 1 and gidx - 32 < 3:
                                m = gidx - 32
                                mp = 31 - ((31 - (2 + m)) % 3)
                                SG.wait(sync, 2 * mp + 1)
                            SG.wait(sync, 2 * (16 * w + oc) + 1)
                            WS.dma(sync,
                                   wsl[gidx % 3].ap().rearrange(
                                       "p (hk i) -> p hk i", hk=HK),
                                   oblk(cw)[oc], gidx - 32)

            @blk.scalar
            def _(sc):
                bw(sc)

                def cast(n):
                    g, h = divmod(n, 2)
                    BS.wait(sc, n)
                    if n >= 6:
                        SG.wait(sc, n - 6)
                    if g >= 3:
                        sc.wait_ge(sem["pew"], g - 2)
                    sc.activation(out=wsl[g % 3][:, 1024 * h:1024 * h + 1024],
                                  in_=c0f[n % 2][:],
                                  func=AF.Identity).then_inc(sem["acc"], 1)
                for s in range(NSEG):
                    for hk in range(HK):
                        n = 16 * s + hk
                        nx = 32 * s + hk
                        XS.wait(sc, nx)
                        if n >= 2:
                            sc.wait_ge(sem["pex"], n - 1)
                        sc.activation(out=sqs[n % 2][:], in_=xsl[nx % 3][:],
                                      func=AF.Square).then_inc(sem["acx"], 1)
                    sc.wait_ge(sem["dvst"], s + 1)
                    sc.activation(out=rrow[:], in_=vrow[:], func=AF.Ln)
                    sc.activation(out=rrow[:], in_=rrow[:], func=AF.Exp,
                                  scale=-0.5).then_inc(sem["acst"], 1)
                    if s == 0:
                        for n in range(4):
                            cast(n)
                    for oc in range(OC):
                        gidx = 32 * s + oc
                        sc.wait_ge(sem["pew"], gidx + 1)
                        sc.activation(out=_sl(kbuf, oc, SEG),
                                      in_=p_kv[gidx % 2][:],
                                      func=AF.Identity).then_inc(sem["dvev"], 1)
                        if s == 0:
                            cast(2 * oc + 4)
                            cast(2 * oc + 5)
                    if s == 0:
                        for n in range(36, 64):
                            cast(n)
            C["acx"] += NSEG * 16
            C["acst"] += NSEG
            C["acc"] += 64

            @blk.tensor
            def _(t):
                bw(t)
                for s in range(NSEG):
                    for hk in range(HK):
                        n = 16 * s + hk
                        nx = 32 * s + hk
                        t.wait_ge(sem["acx"], n + 1)
                        if s >= 1 and hk == 0:
                            t.wait_ge(sem["dvst"], s)
                        st_ = (hk == 0)
                        sp_ = (hk == HK - 1)
                        t.matmul(out=p_sx[:, 0:512], lhsT=cons[:, 0:1],
                                 rhs=xsl[nx % 3][:, 0:512], start=st_, stop=sp_)
                        t.matmul(out=p_sx[:, 512:513], lhsT=cons[:, 0:1],
                                 rhs=xsl[nx % 3][:, 512:513], start=st_, stop=sp_)
                        t.matmul(out=p_sq[:, 0:512], lhsT=cons[:, 0:1],
                                 rhs=sqs[n % 2][:, 0:512], start=st_, stop=sp_)
                        t.matmul(out=p_sq[:, 512:513], lhsT=cons[:, 0:1],
                                 rhs=sqs[n % 2][:, 512:513], start=st_,
                                 stop=sp_).then_inc(sem["pex"], 1)
                    t.wait_ge(sem["dvst"], s + 1)
                    if s >= 1:
                        t.wait_ge(sem["dvbc"], 2 * s)
                    t.matmul(out=p_bc[:, 0:512], lhsT=cons[0:1, :],
                             rhs=mrow[:, 0:512], start=True, stop=True)
                    t.matmul(out=p_bc[:, 512:513], lhsT=cons[0:1, :],
                             rhs=mrow[:, 512:513], start=True,
                             stop=True).then_inc(sem["peb"], 1)
                    t.wait_ge(sem["acst"], s + 1)
                    t.wait_ge(sem["dvbc"], 2 * s + 1)
                    t.matmul(out=p_bc[:, 0:512], lhsT=cons[0:1, :],
                             rhs=rrow[:, 0:512], start=True, stop=True)
                    t.matmul(out=p_bc[:, 512:513], lhsT=cons[0:1, :],
                             rhs=rrow[:, 512:513], start=True,
                             stop=True).then_inc(sem["peb"], 1)
                    for w in range(2):
                        for oc in range(OC):
                            gidx = 32 * s + 16 * w + oc
                            if s == 0:
                                t.wait_ge(sem["acc"],
                                          ACCB + 2 * (16 * w + oc) + 2)
                            else:
                                WS.wait(t, gidx - 32)
                            t.wait_ge(sem["dvmx"], 2 * s + w + 1)
                            if gidx >= 2:
                                t.wait_ge(sem["dvev"], gidx - 1)
                            for hk in range(HK):
                                mm = t.matmul(out=p_kv[gidx % 2][:],
                                              lhsT=_sl(wsl[gidx % 3], hk, 128),
                                              rhs=_sl(xm, hk, SEG),
                                              start=(hk == 0),
                                              stop=(hk == HK - 1))
                            mm.then_inc(sem["pew"], 1)
            C["pex"] += NSEG * 16
            C["peb"] += NSEG * 2
            C["pew"] += NSEG * 32

            @blk.vector
            def _(v):
                bw(v)
                CONV.emit_dve(v, 3)
                for s in range(NSEG):
                    v.wait_ge(sem["pex"], 16 * (s + 1))
                    v.tensor_scalar(out=mrow[:], in0=p_sx[:], scalar1=1.0 / H,
                                    scalar2=None, op0=ALU.mult)
                    v.scalar_tensor_tensor(out=vrow[:], in0=mrow[:],
                                           scalar=-1.0, in1=mrow[:],
                                           op0=ALU.mult, op1=ALU.mult)
                    v.tensor_scalar(out=rrow[:], in0=p_sq[:], scalar1=1.0 / H,
                                    scalar2=EPS, op0=ALU.mult, op1=ALU.add)
                    v.tensor_tensor(out=vrow[:], in0=rrow[:], in1=vrow[:],
                                    op=ALU.add).then_inc(sem["dvst"], 1)
                    v.wait_ge(sem["peb"], 2 * s + 1)
                    v.tensor_copy(out=mb[:], in_=p_bc[:]).then_inc(sem["dvbc"],
                                                                   1)
                    v.wait_ge(sem["peb"], 2 * s + 2)
                    v.tensor_copy(out=rb[:], in_=p_bc[:]).then_inc(sem["dvbc"],
                                                                   1)
                    for hk in range(HK):
                        nx = 32 * s + 16 + hk
                        XS.wait(v, nx)
                        v.tensor_tensor(out=xnf[:], in0=xsl[nx % 3][:],
                                        in1=mb[:], op=ALU.subtract)
                        v.tensor_tensor(out=_sl(xns, hk, SEG + 1), in0=xnf[:],
                                        in1=rb[:],
                                        op=ALU.mult).then_inc(sem["dvn"], 1)
                    if s >= 3:
                        for hk in range(HK):
                            v.tensor_copy(
                                out=_sl(xnm, hk, T + 1, (s - 3) * SEG,
                                        (s - 3) * SEG + SEG + 1),
                                in_=_sl(xns, hk, SEG + 1))
                    def mix(tm, tm1):
                        last = None
                        for hk in range(HK):
                            v.tensor_scalar(out=tsc[:],
                                            in0=_sl(xns, hk, SEG + 1, 1,
                                                    SEG + 1),
                                            scalar1=tm[:, hk:hk + 1],
                                            scalar2=None, op0=ALU.mult)
                            last = v.scalar_tensor_tensor(
                                out=_sl(xm, hk, SEG),
                                in0=_sl(xns, hk, SEG + 1, 0, SEG),
                                scalar=tm1[:, hk:hk + 1], in1=tsc[:],
                                op0=ALU.mult, op1=ALU.add)
                        last.then_inc(sem["dvmx"], 1)
                    mix(tmk, tmk1)
                    CONV.emit_dve(v, 0 if s == 0 else 9)
                    mix(tmv, tmv1)
                    CONV.emit_dve(v, 0 if s == 0 else 4)
                    pend = None
                    for oc in range(OC):
                        gidx = 32 * s + 16 + oc
                        v.wait_ge(sem["pew"], gidx + 1)
                        v.wait_ge(sem["dvev"], 32 * s + oc + 1)
                        if s >= 3:
                            kvdst = _sl(kvm, oc, T, (s - 3) * SEG,
                                        (s - 3) * SEG + SEG)
                        else:
                            kvdst = _sl(kvm, oc, T, 0, SEG)
                        v.tensor_tensor(out=kvdst, in0=_sl(kbuf, oc, SEG),
                                        in1=p_kv[gidx % 2][:],
                                        op=ALU.mult).then_inc(sem["dvev"], 1)
                        v.wait_ge(sem["dvev"], gidx + 1)
                        if s == 3:
                            v.tensor_copy(out=carry3[:, oc:oc + 1],
                                          in_=carry[:, oc:oc + 1])
                        lb = lbf[oc % 2]
                        v.tensor_tensor_scan(
                            out=lb[:, 1:SEG + 1],
                            data0=wcol[:, oc:oc + 1].broadcast_to([128, SEG]),
                            data1=kvdst, initial=carry[:, oc:oc + 1],
                            op0=ALU.mult, op1=ALU.add).then_inc(sem["dvq"], 1)
                        C["dvq"] += 1
                        if pend is not None:
                            po, pt = pend
                            v.wait_ge(sem["dvq"], pt)
                            v.tensor_copy(out=carry[:, po:po + 1],
                                          in_=lbf[po % 2][:, SEG:SEG + 1])
                        pend = (oc, C["dvq"])
                    po, pt = pend
                    v.wait_ge(sem["dvq"], pt)
                    v.tensor_copy(out=carry[:, po:po + 1],
                                  in_=lbf[po % 2][:, SEG:SEG + 1])
                v.tensor_copy(out=carry3[:, 0:1],
                              in_=carry3[:, 0:1]).then_inc(sem["phb"], 1)

            @blk.gpsimd
            def _(g):
                bw(g)
                CONV.emit_gp(g, CONV.ng + 2)
                for n in range(64):
                    w, oc, h = n // 32, (n // 2) % 16, n % 2
                    gsl_ = (n // 2) % 3
                    g.wait_ge(sem["acc"], ACCB + n + 1)
                    SG.dma(g, oblk(cdst[w])[oc][:, 8 * h:8 * h + 8, :],
                           wsl[gsl_][:, 1024 * h:1024 * h + 1024
                                     ].rearrange("p (hk i) -> p hk i", hk=8),
                           n)
                CONV.emit_gp(g, CONV.nd + 1)
            C["dvst"] += NSEG
            C["dvbc"] += NSEG * 2
            C["dvmx"] += NSEG * 2
            C["dvev"] += NSEG * 32
            PHB[0] += 1

        if DEBUG:
            with ExitStack() as phd:
                blkd = phd.enter_context(nc.Block())

                @blkd.gpsimd
                def _(g):
                    bw(g)
                    g.dma_start(out=dbg_xnm, in_=xnm[:]).then_inc(sem["stg"], 16)
                    g.dma_start(out=dbg_kvm, in_=kvm).then_inc(sem["stg"], 16)
                    g.wait_ge(sem["stg"], C["stg"] + 32)
            C["stg"] += 32

        ACC3 = C["acc"]
        PEW3 = C["pew"]
        DVMX3 = C["dvmx"]
        with ExitStack() as ph:
            tsc3 = ph.enter_context(nc.sbuf_tensor("tsc3", [128, T], BF16))
            wf3 = [ph.enter_context(nc.sbuf_tensor(f"wf3{k}", [128, H], F32))
                   for k in range(2)]
            wb3 = [ph.enter_context(nc.sbuf_tensor(f"wb3{k}", [128, H], BF16))
                   for k in range(2)]
            rsg = [ph.enter_context(nc.sbuf_tensor(f"rsg{k}", [128, T], BF16))
                   for k in range(2)]
            t1s = ph.enter_context(nc.sbuf_tensor("t1s", [128, T], F32))
            lb3 = ph.enter_context(nc.sbuf_tensor("lb3", [128, T + 1], F32))
            p_r = [ph.enter_context(nc.psum_tensor(f"p_r{k}", [128, T], F32))
                   for k in range(2)]
            blk = ph.enter_context(nc.Block())

            @blk.sync
            def _(sync):
                bw(sync)
                for oc in range(OC):
                    if oc >= 2:
                        sync.wait_ge(sem["acc"], ACC3 + oc - 1)
                    FS.dma(sync,
                           wf3[oc % 2].ap().rearrange("p (hk i) -> p hk i",
                                                      hk=HK),
                           oblk(d_wr)[oc], oc)

            @blk.scalar
            def _(sc):
                bw(sc)
                def sig(oc):
                    sc.wait_ge(sem["pew"], PEW3 + oc + 1)
                    if oc >= 2:
                        sc.wait_ge(sem["dvat"], oc - 1)
                    sc.activation(out=rsg[oc % 2][:], in_=p_r[oc % 2][:],
                                  func=AF.Sigmoid).then_inc(sem["acs"], 1)
                for oc in range(OC):
                    if oc >= 2:
                        sig(oc - 2)
                        sc.wait_ge(sem["pew"], PEW3 + oc - 1)
                    FS.wait(sc, oc)
                    sc.activation(out=wb3[oc % 2][:], in_=wf3[oc % 2][:],
                                  func=AF.Identity).then_inc(sem["acc"], 1)
                sig(OC - 2)
                sig(OC - 1)
            C["acc"] += 16
            C["acs"] += 16

            @blk.tensor
            def _(t):
                bw(t)
                for oc in range(OC):
                    t.wait_ge(sem["acc"], ACC3 + oc + 1)
                    t.wait_ge(sem["dvmx"], DVMX3 + 1)
                    if oc >= 2:
                        t.wait_ge(sem["acs"], oc - 1)
                    for hk in range(HK):
                        lhsT = _sl(wb3[oc % 2], hk, 128)
                        t.matmul(out=p_r[oc % 2][:, 0:512], lhsT=lhsT,
                                 rhs=_sl(xmr, hk, T, 0, 512),
                                 start=(hk == 0), stop=(hk == HK - 1))
                        mm = t.matmul(out=p_r[oc % 2][:, 512:1024], lhsT=lhsT,
                                      rhs=_sl(xmr, hk, T, 512, 1024),
                                      start=(hk == 0), stop=(hk == HK - 1))
                    mm.then_inc(sem["pew"], 1)
            C["pew"] += 16

            @blk.vector
            def _(v):
                bw(v)
                last = None
                for hk in range(HK):
                    v.tensor_scalar(out=tsc3[:],
                                    in0=_sl(xnm, hk, T + 1, 1, T + 1),
                                    scalar1=tmr[:, hk:hk + 1], scalar2=None,
                                    op0=ALU.mult)
                    last = v.scalar_tensor_tensor(
                        out=_sl(xmr, hk, T), in0=_sl(xnm, hk, T + 1, 0, T),
                        scalar=tmr1[:, hk:hk + 1], in1=tsc3[:],
                        op0=ALU.mult, op1=ALU.add)
                last.then_inc(sem["dvmx"], 1)
                for oc in range(OC):
                    v.tensor_copy(out=lb3[:, 0:1], in_=carry3[:, oc:oc + 1])
                    v.tensor_tensor_scan(
                        out=lb3[:, 1:T + 1],
                        data0=wcol[:, oc:oc + 1].broadcast_to([128, T]),
                        data1=_sl(kvm, oc, T), initial=carry3[:, oc:oc + 1],
                        op0=ALU.mult, op1=ALU.add).then_inc(sem["dvq"], 1)
                    C["dvq"] += 1
                    v.wait_ge(sem["dvq"], C["dvq"])
                    v.scalar_tensor_tensor(out=t1s[:], in0=_sl(kvm, oc, T),
                                           scalar=ucol[:, oc:oc + 1],
                                           in1=lb3[:, 0:T],
                                           op0=ALU.mult, op1=ALU.add)
                    v.wait_ge(sem["acs"], oc + 1)
                    v.tensor_tensor(out=_sl(kvm, oc, T), in0=t1s[:],
                                    in1=rsg[oc % 2][:],
                                    op=ALU.mult).then_inc(sem["dvat"], 1)
                    CONV.emit_dve(v, 1)
                v.tensor_copy(out=carry3[:, 0:1],
                              in_=carry3[:, 0:1]).then_inc(sem["phb"], 1)

            @blk.gpsimd
            def _(g):
                bw(g)
                CONV.emit_gp(g, CONV.nd + 1)
            C["dvmx"] += 1
            C["dvat"] += 16
            PHB[0] += 1

        if DEBUG:
            with ExitStack() as phd:
                blkd = phd.enter_context(nc.Block())

                @blkd.gpsimd
                def _(g):
                    bw(g)
                    g.dma_start(out=dbg_attn, in_=kvm).then_inc(sem["stg"], 16)
                    g.wait_ge(sem["stg"], C["stg"] + 16)
            C["stg"] += 16

        ACC4 = C["acc"]
        PEW4 = C["pew"]
        DVH4 = C["dvh"]
        with ExitStack() as ph:
            wf4 = [ph.enter_context(nc.sbuf_tensor(f"wf4{k}", [128, H], F32))
                   for k in range(2)]
            wb4 = [ph.enter_context(nc.sbuf_tensor(f"wb4{k}", [128, H], BF16))
                   for k in range(2)]
            p_s = [ph.enter_context(nc.psum_tensor(f"p_s{k}", [128, T], F32))
                   for k in range(2)]
            blk = ph.enter_context(nc.Block())

            @blk.sync
            def _(sync):
                bw(sync)
                for hc in range(HK):
                    if hc >= 2:
                        sync.wait_ge(sem["acc"], ACC4 + hc - 1)
                    FS.dma(sync,
                           wf4[hc % 2].ap().rearrange("p (hk i) -> p hk i",
                                                      hk=HK),
                           oblk(d_wo)[hc], 16 + hc)

            @blk.scalar
            def _(sc):
                bw(sc)
                for hc in range(HK):
                    FS.wait(sc, 16 + hc)
                    if hc >= 2:
                        sc.wait_ge(sem["pew"], PEW4 + hc - 1)
                    sc.activation(out=wb4[hc % 2][:], in_=wf4[hc % 2][:],
                                  func=AF.Identity).then_inc(sem["acc"], 1)
            C["acc"] += 16

            @blk.tensor
            def _(t):
                bw(t)
                for hc in range(HK):
                    t.wait_ge(sem["acc"], ACC4 + hc + 1)
                    if hc >= 2:
                        t.wait_ge(sem["dvh"], DVH4 + hc - 1)
                    for oc in range(OC):
                        lhsT = _sl(wb4[hc % 2], oc, 128)
                        t.matmul(out=p_s[hc % 2][:, 0:512], lhsT=lhsT,
                                 rhs=_sl(kvm, oc, T, 0, 512),
                                 start=(oc == 0), stop=(oc == OC - 1))
                        mm = t.matmul(out=p_s[hc % 2][:, 512:1024], lhsT=lhsT,
                                      rhs=_sl(kvm, oc, T, 512, 1024),
                                      start=(oc == 0), stop=(oc == OC - 1))
                    mm.then_inc(sem["pew"], 1)
            C["pew"] += 16

            @blk.vector
            def _(v):
                bw(v)
                for hc in range(HK):
                    v.wait_ge(sem["pew"], PEW4 + hc + 1)
                    v.tensor_copy(out=_sl(sout, hc, T),
                                  in_=p_s[hc % 2][:]).then_inc(sem["dvh"], 1)
                    CONV.emit_dve(v, 1)
                v.tensor_copy(out=carry3[:, 0:1],
                              in_=carry3[:, 0:1]).then_inc(sem["phb"], 1)

            @blk.gpsimd
            def _(g):
                bw(g)
                CONV.emit_gp(g, CONV.nd + 1)
            C["dvh"] += 16
            PHB[0] += 1

        PEWB = C["pew"]
        ACSB = C["acs"]
        DVHB = C["dvh"]

        def pew_gu(th, ic, w):
            return PEWB + th * 102 + 2 * ic + w + 1

        def pew_d(th, hc):
            return PEWB + th * 102 + 86 + hc + 1

        def acs_gu(th, ic):
            return ACSB + th * 59 + ic + 1

        def acs_d(th, hc):
            return ACSB + th * 59 + 43 + hc + 1

        def dvh_gu(th, ic):
            return DVHB + th * 59 + ic + 1

        def dvh_d(th, hc):
            return DVHB + th * 59 + 43 + hc + 1

        with ExitStack() as ph:
            gsl = [ph.enter_context(nc.sbuf_tensor(f"gsl{k}", [128, H], BF16))
                   for k in range(2)]
            usl = [ph.enter_context(nc.sbuf_tensor(f"usl{k}", [128, H], BF16))
                   for k in range(2)]
            hgs = [ph.enter_context(nc.sbuf_tensor(f"hgs{k}", [128, SEG], F32))
                   for k in range(2)]
            dsl = [ph.enter_context(
                nc.sbuf_tensor(f"dsl{k}", [128, IC * 128], BF16))
                for k in range(2)]
            xs6 = [ph.enter_context(nc.sbuf_tensor(f"xs6{k}", [128, SEG], F32))
                   for k in range(2)]
            gs6 = [ph.enter_context(nc.sbuf_tensor(f"gs6{k}", [128, SEG], F32))
                   for k in range(2)]
            ob6 = [ph.enter_context(nc.sbuf_tensor(f"ob6{k}", [128, SEG], F32))
                   for k in range(2)]
            ts6 = ph.enter_context(nc.sbuf_tensor("ts6", [128, SEG], F32))
            p_g = [ph.enter_context(nc.psum_tensor(f"p_g{k}", [128, SEG], F32))
                   for k in range(2)]
            p_u = [ph.enter_context(nc.psum_tensor(f"p_u{k}", [128, SEG], F32))
                   for k in range(2)]
            p_d = [ph.enter_context(nc.psum_tensor(f"p_d{k}", [128, SEG], F32))
                   for k in range(2)]
            blk = ph.enter_context(nc.Block())

            @blk.sync
            def _(sync):
                bw(sync)
                for th in range(2):
                    for ic in range(IC):
                        gi = th * IC + ic
                        if gi >= 2:
                            t2, i2 = divmod(gi - 2, IC)
                            sync.wait_ge(sem["pew"], pew_gu(t2, i2, 1))
                        L = CONV.last[("u", ic)]
                        CVS.wait(sync, L // 2)
                        spin(sync, L // 2 + 1)
                        GS.dma(sync,
                               gsl[gi % 2].ap().rearrange(
                                   "p (hk i) -> p hk i", hk=HK),
                               wsv[:, 16 * ic:16 * ic + 16, :], gi)
                        US.dma(sync,
                               usl[gi % 2].ap().rearrange(
                                   "p (hk i) -> p hk i", hk=HK),
                               wsv[:, 688 + 16 * ic:688 + 16 * ic + 16, :], gi)
                    for hc in range(HK):
                        gj = th * HK + hc
                        if gj >= 2:
                            t2, h2 = divmod(gj - 2, HK)
                            sync.wait_ge(sem["pew"], pew_d(t2, h2))
                            sync.wait_ge(sem["dvh"], dvh_d(t2, h2))
                        L = CONV.last[("d", hc)]
                        CVS.wait(sync, L // 2)
                        spin(sync, L // 2 + 1)
                        dv3 = dsl[gj % 2].ap().rearrange(
                            "p (ic j) -> p ic j", ic=IC)
                        DSm.dma(sync, dv3[:, 0:22, :],
                                wsv[:, 1376 + 43 * hc:1376 + 43 * hc + 22, :],
                                2 * gj)
                        DSm.dma(sync, dv3[:, 22:43, :],
                                wsv[:, 1376 + 43 * hc + 22:
                                    1376 + 43 * hc + 43, :],
                                2 * gj + 1)
                        YS.dma(sync, xs6[gj % 2][:],
                               xr[hc, :, W + 1 + SEG * th:
                                  W + 1 + SEG * (th + 1)], gj)

            @blk.tensor
            def _(t):
                bw(t)
                for th in range(2):
                    TH0 = 1 + SEG * th
                    for ic in range(IC):
                        gi = th * IC + ic
                        GS.wait(t, gi)
                        if gi >= 2:
                            t2, i2 = divmod(gi - 2, IC)
                            t.wait_ge(sem["acs"], acs_gu(t2, i2))
                        for hk in range(HK):
                            mm = t.matmul(out=p_g[gi % 2][:],
                                          lhsT=_sl(gsl[gi % 2], hk, 128),
                                          rhs=_sl(xnm, hk, T + 1, TH0,
                                                  TH0 + SEG),
                                          start=(hk == 0), stop=(hk == HK - 1))
                        mm.then_inc(sem["pew"], 1)
                        US.wait(t, gi)
                        if gi >= 2:
                            t2, i2 = divmod(gi - 2, IC)
                            t.wait_ge(sem["dvh"], dvh_gu(t2, i2))
                        for hk in range(HK):
                            mm = t.matmul(out=p_u[gi % 2][:],
                                          lhsT=_sl(usl[gi % 2], hk, 128),
                                          rhs=_sl(xnm, hk, T + 1, TH0,
                                                  TH0 + SEG),
                                          start=(hk == 0), stop=(hk == HK - 1))
                        mm.then_inc(sem["pew"], 1)
                    t.wait_ge(sem["dvh"], dvh_gu(th, IC - 1))
                    for hc in range(HK):
                        gj = th * HK + hc
                        DSm.wait(t, 2 * gj)
                        if gj >= 2:
                            t2, h2 = divmod(gj - 2, HK)
                            t.wait_ge(sem["acs"], acs_d(t2, h2))
                        for ic in range(IC):
                            if ic == 22:
                                DSm.wait(t, 2 * gj + 1)
                            mm = t.matmul(out=p_d[gj % 2][:],
                                          lhsT=_sl(dsl[gj % 2], ic, 128),
                                          rhs=_sl(hact, ic, SEG),
                                          start=(ic == 0), stop=(ic == IC - 1))
                        mm.then_inc(sem["pew"], 1)
            C["pew"] += 2 * (2 * IC + HK)

            @blk.scalar
            def _(sc):
                bw(sc)
                for th in range(2):
                    for ic in range(IC):
                        gi = th * IC + ic
                        sc.wait_ge(sem["pew"], pew_gu(th, ic, 0))
                        if gi >= 2:
                            t2, i2 = divmod(gi - 2, IC)
                            sc.wait_ge(sem["dvh"], dvh_gu(t2, i2))
                        sc.activation(out=hgs[gi % 2][:], in_=p_g[gi % 2][:],
                                      func=AF.Silu).then_inc(sem["acs"], 1)
                    for hc in range(HK):
                        gj = th * HK + hc
                        sc.wait_ge(sem["pew"], pew_d(th, hc))
                        if gj >= 2:
                            t2, h2 = divmod(gj - 2, HK)
                            sc.wait_ge(sem["dvh"], dvh_d(t2, h2))
                        sc.activation(out=gs6[gj % 2][:], in_=p_d[gj % 2][:],
                                      func=AF.Silu).then_inc(sem["acs"], 1)
            C["acs"] += 2 * (IC + HK)

            @blk.vector
            def _(v):
                bw(v)
                for th in range(2):
                    if th == 1:
                        v.wait_ge(sem["pew"], pew_d(0, HK - 1))
                    for ic in range(IC):
                        gi = th * IC + ic
                        CONV.emit_dve(v, 4)
                        v.wait_ge(sem["acs"], acs_gu(th, ic))
                        v.wait_ge(sem["pew"], pew_gu(th, ic, 1))
                        v.tensor_tensor(out=_sl(hact, ic, SEG),
                                        in0=hgs[gi % 2][:],
                                        in1=p_u[gi % 2][:],
                                        op=ALU.mult).then_inc(sem["dvh"], 1)
                    for hc in range(HK):
                        gj = th * HK + hc
                        v.wait_ge(sem["acs"], acs_d(th, hc))
                        YS.wait(v, gj)
                        if gj >= 2:
                            OS.wait(v, gj - 2)
                        v.tensor_tensor(out=ts6[:], in0=gs6[gj % 2][:],
                                        in1=_sl(sout, hc, T, SEG * th,
                                                SEG * (th + 1)),
                                        op=ALU.mult)
                        v.tensor_tensor(out=ob6[gj % 2][:], in0=ts6[:],
                                        in1=xs6[gj % 2][:],
                                        op=ALU.add).then_inc(sem["dvh"], 1)
            C["dvh"] += 2 * (IC + HK)

            @blk.gpsimd
            def _(g):
                bw(g)
                CONV.emit_gp(g, CONV.nd + 1)
                if CONV.nd >= NTL and not CONV.flushed:
                    CONV.flushed = True
                    CONV.flush_gp(g)
                for th in range(2):
                    for hc in range(HK):
                        gj = th * HK + hc
                        g.wait_ge(sem["dvh"], dvh_d(th, hc))
                        OS.dma(g, outr[hc][:, SEG * th:SEG * (th + 1)],
                               ob6[gj % 2][:], gj)
                OS.wait_all(g, 2 * HK)
            PHB[0] += 0
        if DEBUG:
            with ExitStack() as phd:
                blkd = phd.enter_context(nc.Block())

                @blkd.gpsimd
                def _(g):
                    bw(g)
                    g.dma_start(out=dbg_s, in_=sout).then_inc(sem["stg"], 16)
                    g.wait_ge(sem["stg"], C["stg"] + 16)
            C["stg"] += 16
    return nc


def kernel(L0_attn_hidden_states, L0_attn_Wr, L0_attn_Wk, L0_attn_Wv,
           L0_attn_Wo, L0_attn_time_decay, L0_attn_time_first,
           L0_attn_time_mix_k, L0_attn_time_mix_v, L0_attn_time_mix_r,
           L0_ffn_lut, L0_ffn_gate_idx, L0_ffn_up_idx, L0_ffn_down_idx):
    x = np.asarray(L0_attn_hidden_states, dtype=np.float32)
    Wr = np.asarray(L0_attn_Wr, dtype=np.float32)
    Wk = np.asarray(L0_attn_Wk, dtype=np.float32)
    Wv = np.asarray(L0_attn_Wv, dtype=np.float32)
    Wo = np.asarray(L0_attn_Wo, dtype=np.float32)
    td = np.asarray(L0_attn_time_decay, dtype=np.float32)
    tf = np.asarray(L0_attn_time_first, dtype=np.float32)
    tmk = np.asarray(L0_attn_time_mix_k, dtype=np.float32).reshape(-1)
    tmv = np.asarray(L0_attn_time_mix_v, dtype=np.float32).reshape(-1)
    tmr = np.asarray(L0_attn_time_mix_r, dtype=np.float32).reshape(-1)
    lut = np.asarray(L0_ffn_lut, dtype=np.float32).reshape(-1)
    gidx = np.asarray(L0_ffn_gate_idx, dtype=np.int32)
    uidx = np.asarray(L0_ffn_up_idx, dtype=np.int32)
    didx = np.asarray(L0_ffn_down_idx, dtype=np.int32)

    gidxT = np.ascontiguousarray(gidx.T)
    uidxT = np.ascontiguousarray(uidx.T)
    didxT = np.ascontiguousarray(didx.T)
    gidx_blk = np.ascontiguousarray(
        gidxT.reshape(H, IC, 128).transpose(1, 0, 2))
    uidx_blk = np.ascontiguousarray(
        uidxT.reshape(H, IC, 128).transpose(1, 0, 2))
    didx_blk = np.ascontiguousarray(
        didxT.reshape(I5504, HK, 128).transpose(1, 0, 2))

    allt = np.empty((NT, 128, 1024), np.int32)
    for ic in range(IC):
        for r in range(4):
            fam = gidx_blk if r < 2 else uidx_blk
            h2 = r % 2
            blkv = fam[ic].reshape(16, 128, 128)[8 * h2:8 * h2 + 8]
            allt[4 * ic + r] = blkv.transpose(1, 0, 2).reshape(128, 1024)
    dflat = didx_blk.reshape(HK * IC, 128, 128)
    for m in range(NT - 172):
        allt[172 + m] = dflat[8 * m:8 * m + 8].transpose(1, 0, 2
                                                         ).reshape(128, 1024)
    cidx_par = [np.ascontiguousarray(allt[q::2].astype(np.int8))
                for q in range(2)]
    ptab = (BM + np.arange(NTL + 2, dtype=np.int32)).reshape(1, -1)

    def blkT(Wm):
        return np.ascontiguousarray(Wm.reshape(16, 128, H).transpose(0, 2, 1))

    wk_blk = blkT(Wk)
    wv_blk = blkT(Wv)
    wr_blk = blkT(Wr)
    wo_blk = blkT(Wo)
    tmk_c = np.ascontiguousarray(tmk.reshape(16, 128).T)
    tmv_c = np.ascontiguousarray(tmv.reshape(16, 128).T)
    tmr_c = np.ascontiguousarray(tmr.reshape(16, 128).T)
    u_c = np.ascontiguousarray(tf.reshape(16, 128).T)
    td_c = np.ascontiguousarray(td.reshape(16, 128).T)
    ones = np.ones((128, 128), np.float32)

    in2 = []
    for c in range(8):
        b, p = divmod(c, 4)
        xt = x[b].T
        xw = np.zeros((H, COLS), np.float32)
        t0 = T * p - (W + 1)
        lo = max(0, -t0)
        xw[:, lo:] = xt[:, t0 + lo:t0 + COLS]
        in2.append({
            "x_fm": np.ascontiguousarray(xw),
            "wk_blk": wk_blk, "wv_blk": wv_blk, "wr_blk": wr_blk,
            "wo_blk": wo_blk, "cidx": cidx_par[c & 1],
            "pmeta": np.array([[c & 1, 0, 0, 0]], np.int32), "ptab": ptab,
            "tmk": tmk_c, "tmv": tmv_c, "tmr": tmr_c,
            "ucol": u_c, "tdcol": td_c, "ones": ones,
        })
    nc2 = build_main_nc(lut)
    res2 = run_bass_kernel_spmd(nc2, in2, core_ids=list(range(8)))

    out = np.empty((2, 4096, H), np.float32)
    for c in range(8):
        b, p = divmod(c, 4)
        out[b, T * p:T * (p + 1), :] = res2.results[c]["out_fm"].T
    return out
